# version 01
# baseline (speedup 1.0000x reference)
import dataclasses
import numpy as np
import ml_dtypes
from contextlib import ExitStack

import concourse.bass as bass
import concourse.mybir as mybir
from concourse.bass_utils import run_bass_kernel_spmd

F32 = mybir.dt.float32
BF16 = mybir.dt.bfloat16
AF = mybir.ActivationFunctionType
ALU = mybir.AluOpType
AX = mybir.AxisListType

B = 8
L = 4096
D = 2048
NKC = 16
DIN = 4096
NH, HD, DST = 64, 64, 128
CONVD = 4352
NCONV = 34
NGATE = 32
CONVK = 4
CH = 128
F_E = 1408
NF = 11
FS = 2816
NE = 8
NPE = 10
TT = 1024
NTT = L // TT
NSUB = TT // 128


class Sem:
    def __init__(self, h, name=""):
        self.h = h; self.n = 0; self.name = name

CHAINED = ("gpsimd", "vector", "scalar")

class Plan:
    ENGINES = ("sync", "gpsimd", "tensor", "vector", "scalar")

    def __init__(self, mksem):
        self.q = {e: [] for e in self.ENGINES}
        self.chain = {e: Sem(mksem(f"chain_{e}"), f"chain_{e}") for e in CHAINED}
        self.last = {}
        self.n_instr = 0

    def emit(self, e, fn, inc=None, k=1, chain=True):
        q = self.q[e]; self.n_instr += 1
        if e in CHAINED:
            if chain and e in self.last:
                h, v = self.last[e]
                q.append(lambda eng, h=h, v=v: eng.wait_ge(h, v))
            if inc is None:
                s = self.chain[e]; s.n += 1; kk = 1
            else:
                s = inc; s.n += k; kk = k
            q.append(lambda eng, fn=fn, h=s.h, kk=kk: fn(eng).then_inc(h, kk))
            self.last[e] = (s.h, s.n)
        else:
            if inc is not None:
                inc.n += k
                q.append(lambda eng, fn=fn, h=inc.h, kk=k: fn(eng).then_inc(h, kk))
            else:
                q.append(fn)

    def dma(self, e, fn, sem):
        self.n_instr += 1
        sem.n += 16
        self.q[e].append(lambda eng, fn=fn, h=sem.h: fn(eng).then_inc(h, 16))

    def wait(self, e, sem, v=None):
        v = sem.n if v is None else v
        if v > 0:
            self.q[e].append(lambda eng, h=sem.h, v=v: eng.wait_ge(h, v))

    def mark(self, e):
        return self.last.get(e)

    def wait_mark(self, e_waiter, mark):
        if mark is not None:
            h, v = mark
            self.q[e_waiter].append(lambda eng, h=h, v=v: eng.wait_ge(h, v))

    def run(self, block):
        for e in self.ENGINES:
            if self.q[e]:
                getattr(block, e)(lambda eng, ops=tuple(self.q[e]): [f(eng) for f in ops])


def bcast(ap, idx, count):
    l = [list(x) for x in ap.ap]
    l.insert(idx + 1, [0, count])
    return dataclasses.replace(ap, ap=l)


def bcast_last(ap, count):
    l = [list(x) for x in ap.ap] + [[0, count]]
    return dataclasses.replace(ap, ap=l)


class Carver:
    def __init__(self, arena_handle, nbytes):
        self.t = arena_handle
        self.nbytes = nbytes
        self.watermark = 0

    def at(self, off, parts, cols, dt, name=""):
        size = mybir.dt.size(dt)
        assert off % 32 == 0, (name, off)
        assert off + cols * size <= self.nbytes, (name, off, cols, size, self.nbytes)
        self.watermark = max(self.watermark, off + cols * size)
        ap = self.t.ap()[0:parts, off:off + cols * size].bitcast(dt)
        return ap

    def at3(self, off, parts, a, b, dt, name=""):
        return self.at(off, parts, a * b, dt, name).rearrange(
            "p (a b) -> p a b", a=a)


def pe_sem_mark(sem):
    return (sem.h, sem.n)


def tpose(t, out, in_, identity):
    return t.matmul(out, in_, identity, start=True, stop=True, is_transpose=True)



def build_program(dev_tiles=None, dev_phases=None, extra_taps=(), dev_parts=None):
    tiles = list(range(NTT)) if dev_tiles is None else list(dev_tiles)
    phases = ("ffn1", "mixer", "ffn2") if dev_phases is None else tuple(dev_phases)

    nc = bass.Bass("TRN2", target_bir_lowering=False)
    es = ExitStack()

    def din(name, shape, dt=F32):
        return nc.dram_tensor(name, list(shape), dt, kind="ExternalInput").ap()

    x_in  = din("x", (L, D))
    pk_wsg = din("pk_wsg", (22, D, 128))
    pk_wsu = din("pk_wsu", (22, D, 128))
    pk_wg  = din("pk_wg", (NE, NF, D, 128))
    pk_wu  = din("pk_wu", (NE, NF, D, 128))
    pk_wsd = din("pk_wsd", (22, 128, D))
    pk_wd  = din("pk_wd", (NE, NF, 128, D))
    pk_wr  = din("pk_wr", (D, NE))
    pk_inx = din("pk_inx", (NCONV, D, 128))
    pk_ing = din("pk_ing", (NGATE, D, 128))
    pk_indt = din("pk_indt", (D, NH))
    pk_op  = din("pk_op", (NGATE, 128, D))
    convw_d = din("convw", (CONVD, CONVK))
    convb_d = din("convb", (CONVD, 1))
    dtb_d  = din("dtb", (NH, 1))
    alog_d = din("alog", (NH, 1))
    dvec_d = din("dvec", (1, NH))
    nw_d   = din("nw", (1, DIN))
    id16_d = din("id16", (128, 128), BF16)
    idf_d  = din("idf", (128, 128))
    tril_d = din("tril", (128, 128))
    oh8_d  = din("oh8", (NE, NE * 128))

    out_d = nc.dram_tensor("out", [L, D], F32, kind="ExternalOutput").ap()

    def dscr(name, shape, dt=F32):
        return nc.dram_tensor(name, list(shape), dt, kind="Internal").ap()

    xmid1 = dscr("xmid1", (L, D))
    w_scr = dscr("w_scr", (NE, TT))
    xmid2 = dscr("xmid2", (L, D))
    rowscr = dscr("rowscr", (2, NH, CH))
    rowlast = dscr("rowlast", (2, NH, 1))
    gate_scr = dscr("gate_scr", (NGATE, 128, TT), BF16)
    yssd_scr = dscr("yssd_scr", (NSUB, 128, DIN), BF16)
    N_UP = 22 + 22 + NE * NF + NE * NF + NCONV + NGATE
    cache_up = dscr("cache_up", (N_UP, 128, NKC * 128), BF16)
    cache_dt = dscr("cache_dt", (1, 128, NKC * NH), BF16)
    cache_dn = dscr("cache_dn", (22 + NE * NF + NGATE, 128, D), BF16)

    avail = (nc.sbuf_bytes_remaining - 64) // 32 * 32
    arena = es.enter_context(nc.sbuf_tensor("arena", [128, avail], mybir.dt.uint8))
    C = Carver(arena, avail)

    cur = [0]
    def take(nbytes, align=32):
        cur[0] = (cur[0] + align - 1) // align * align
        off = cur[0]; cur[0] += nbytes
        assert cur[0] <= avail, (off, nbytes, cur[0], avail)
        return off

    cI16  = C.at(take(256), 128, 128, BF16, "cI16")
    cIf   = C.at(take(512), 128, 128, F32, "cIf")
    ctril = C.at(take(512), 128, 128, F32, "ctril")
    cpool = C.at(take(32), 128, 8, F32, "cpool")
    convw = C.at3(take(544), 128, NCONV, CONVK, F32, "convw")
    convb = C.at3(take(160), 128, NCONV, 1, F32, "convb")
    dD    = C.at(take(256), 128, NH, F32, "dD")
    wrsb  = C.at3(take(256), 128, NKC, NE, BF16, "wrsb")
    oh8   = C.at(take(4096), NE, NE * 128, F32, "oh8")
    dtb   = C.at(take(32), NH, 1, F32, "dtb")
    negA  = C.at(take(32), NH, 1, F32, "negA")
    carry = C.at3(take(224), 128, NCONV, 3, BF16, "carry")
    yfm   = C.at3(take(32768), 128, NKC, TT, BF16, "yfm")
    xln   = C.at(take(8192), 128, D, F32, "xln")
    ytmb  = C.at(take(4096), 128, D, BF16, "ytmb")
    stats = C.at(take(128), 128, 32, F32, "stats")
    mv    = C.at(take(32), 128, 2, F32, "mv")
    rstd  = C.at(take(32), 128, 1, F32, "rstd")
    U0 = (cur[0] + 31) // 32 * 32

    cur[0] = U0
    racc  = C.at3(take(65536), 128, NSUB, D, F32, "racc")
    hreg  = C.at3(take(22528), 128, NF, TT, BF16, "hreg")
    wdh   = [C.at(take(4096), 128, D, BF16, f"wdh{i}") for i in range(6)]
    wbsb  = C.at(take(4096), 128, TT, F32, "wbsb")
    upw   = [C.at3(take(4096), 128, NKC, 128, BF16, f"upw{i}") for i in range(4)]
    sgscr = C.at(take(4096), 128, TT, F32, "sgscr")
    zfm   = C.at(take(4096), NE, TT, F32, "zfm")
    ztm   = C.at(take(64), 128, NE, F32, "ztm")
    top8  = C.at(take(32), 128, NE, F32, "top8")
    wtm   = C.at(take(64), 128, NE, F32, "wtm")
    wfm   = C.at(take(4096), NE, TT, F32, "wfm")
    nm1   = C.at(take(32), 128, 1, F32, "nm1")
    sume  = C.at(take(32), 128, 1, F32, "sume")
    rmask = C.at(take(64), 128, NE, F32, "rmask")
    ffn_end = cur[0]

    cur[0] = U0
    HSTR = 1028
    off_hwork = take(2 * 35 * HSTR)
    hwork = C.at3(off_hwork, 128, 35, HSTR, BF16, "hwork")
    Rst   = C.at(take(16384), 128, DIN, F32, "Rst")
    off_yssdb = take(8192); yssdb = C.at(off_yssdb, 128, DIN, BF16, "yssdb")
    off_xT = take(8192);  xT  = C.at(off_xT, 128, DIN, BF16, "xT")
    off_xdt = take(8192); xdt = C.at(off_xdt, 128, DIN, BF16, "xdt")
    xw    = C.at(off_xdt, 128, DIN, BF16, "xw")
    off_rep = [take(4096), take(4096)]
    rep   = [C.at(off_rep[i], 128, 8 * CH, F32, f"rep{i}") for i in range(2)]
    off_dsb = take(4096)
    dsb   = C.at(off_dsb, 128, 8 * CH, F32, "dsb")
    off_glb = [take(2048), take(2048)]
    glb   = [C.at(off_glb[i], 128, 8 * CH, BF16, f"glb{i}") for i in range(2)]
    off_miw = [take(4096), take(4096)]
    miw   = [C.at3(off_miw[i], 128, NKC, 128, BF16, f"miw{i}") for i in range(2)]
    off_acs = take(512);  acs  = C.at(off_acs, NH, CH, F32, "acs")
    acs2  = C.at(take(512), NH, CH, F32, "acs2")
    sdt   = C.at(take(512), NH, CH, F32, "sdt")
    wst   = C.at(take(512), NH, CH, F32, "wst")
    spt   = C.at(take(512), NH, CH, F32, "spt")
    acsT  = C.at(take(256), 128, NH, F32, "acsT")
    dtT   = C.at(take(256), 128, NH, F32, "dtT")
    wstT  = C.at(take(256), 128, NH, F32, "wstT")
    eaT   = C.at(take(256), 128, NH, F32, "eaT")
    easum = C.at(take(256), 128, NH, F32, "easum")
    gmask = C.at(take(512), 128, 128, F32, "gmask")
    off_bT = take(256); bT = C.at(off_bT, 128, 128, BF16, "bT")
    rbg   = C.at(take(2048), 128, 1024, BF16, "rbg")
    yevt  = C.at(take(2048), 128, 512, F32, "yevt")
    dres  = C.at(take(2048), 128, 512, F32, "dres")
    rupd  = C.at(take(2048), 128, 512, F32, "rupd")
    off_convacc = take(4096); convacc = C.at(off_convacc, 128, TT, F32, "convacc")
    off_dtraw = take(4096); dtraw = C.at(off_dtraw, NH, TT, F32, "dtraw")
    nwfm  = C.at(take(128), 128, NGATE, F32, "nwfm")
    mix_end = cur[0]
    ynfm  = C.at3(off_hwork, 128, NGATE, TT, BF16, "ynfm")
    ysr   = [C.at(off_xT, 128, DIN, BF16, "ysr0"),
             C.at(off_xdt, 128, DIN, BF16, "ysr1")]
    sgu   = [C.at3(off_rep[0], 128, NGATE, 128, BF16, "sgu0"),
             C.at3(off_dsb, 128, NGATE, 128, BF16, "sgu1")]
    sgT   = C.at(off_yssdb, 128, DIN, BF16, "sgT")
    yfb   = C.at(off_miw[0], 128, DIN, BF16, "yfb")
    sqs   = C.at(off_convacc, 128, DIN, BF16, "sqs")
    s2c   = C.at(off_bT, 128, 1, F32, "s2c")
    rstd2 = C.at(off_bT + 32, 128, 1, F32, "rstd2")
    opbuf = [C.at(off_glb[0], 128, D, BF16, "opbuf0"),
             C.at(off_rep[1], 128, D, BF16, "opbuf1")]
    xep2  = C.at(off_yssdb, 128, D, F32, "xep2")
    assert max(ffn_end, mix_end) <= avail, (ffn_end, mix_end, avail)

    psum = es.enter_context(nc.psum_tensor("psum", [128, 4096], F32))
    PS = psum.ap()
    PSB = psum.ap().bitcast(BF16)
    psg  = [PS[:, 0:1024], PS[:, 2048:3072]]
    psu  = [PS[:, 1024:2048], PS[:, 3072:4096]]
    psd  = [PS[:, 0:2048], PS[:, 2048:4096]]
    pstr = PSB[:, 0:4096]
    psz  = PS[0:NE, 1024:2048]
    pszt = PS[:, 2048:2048 + NE]
    pswt = PS[0:NE, 2560:2688]
    pswb = PS[:, 3072:4096]
    psx  = [PS[:, 0:1024], PS[:, 1024:2048]]
    psdt = PS[0:NH, 2048:3072]
    psYA = PS[:, 0:512]
    psYB = PS[:, 512:1024]
    psS  = PS[:, 1024:1536]
    psG  = PS[:, 1536:2048]
    pstr2 = PSB[:, 4096:6144]
    psTa = PS[:, 3072:3136]
    psTb = PS[:, 3136:3200]
    psTc = PS[:, 3200:3264]

    def mksem(name):
        return es.enter_context(nc.semaphore(name))
    P = Plan(mksem)
    ALL_DMA_SEMS = []
    def dsem(name):
        s = Sem(mksem(name))
        ALL_DMA_SEMS.append(s)
        return s
    q_x   = dsem("q_x")
    mkpair = lambda n: (dsem(n + "s"), dsem(n + "h"))
    supw  = [mkpair(f"supw{i}") for i in range(4)]
    swdh  = [mkpair(f"swdh{i}") for i in range(6)]
    smiw  = [mkpair(f"smiw{i}") for i in range(2)]
    sopb  = [mkpair(f"sopb{i}") for i in range(2)]
    q_w   = dsem("q_w")
    q_cr  = dsem("q_cr")
    q_sp  = dsem("q_sp")
    q_rw  = dsem("q_rw")
    q_out = dsem("q_out")
    pem   = Sem(mksem("pem"))

    def phase_barrier():
        marks = [P.mark(e) for e in CHAINED]
        for e in P.ENGINES:
            for s in ALL_DMA_SEMS:
                P.wait(e, s)
            P.wait(e, pem)
            for m in marks:
                P.wait_mark(e, m)

    cache_reg = {}
    cache_next = {"up": 0, "dt": 0, "dn": 0}
    cache_arr = {"up": cache_up, "dt": cache_dt, "dn": cache_dn}
    cache_spill_mark = {}
    slot_spill = {}

    def load_block(key, kind, dst_flat, src_fp32_fn, slot_sem, postproc=None, free_mark=None):
        ssw, shw = slot_sem
        sid = id(ssw)
        if key not in cache_reg:
            P.wait_mark("gpsimd", free_mark)
            P.wait_mark("gpsimd", slot_spill.get(sid))
            slot = cache_next[kind]; cache_next[kind] += 1
            cache_reg[key] = slot
            P.dma("gpsimd", lambda g, d=dst_flat: src_fp32_fn(g, d), ssw)
            ready = (ssw.h, ssw.n)
            if postproc is not None:
                P.wait("vector", Sem(ssw.h), ssw.n)
                postproc()
                ready = P.mark("vector")
            if postproc is not None:
                P.wait_mark("scalar", ready)
            else:
                P.wait("scalar", Sem(ssw.h), ssw.n)
            P.dma("scalar", lambda s, d=dst_flat, a=cache_arr[kind], i=slot:
                  s.dma_start(out=a[i], in_=d), shw)
            cache_spill_mark[key] = (shw.h, shw.n)
            slot_spill[sid] = (shw.h, shw.n)
            return ready
        else:
            slot = cache_reg[key]
            P.wait_mark("sync", free_mark)
            P.wait_mark("sync", slot_spill.get(sid))
            P.wait_mark("sync", cache_spill_mark[key])
            P.dma("sync", lambda s, d=dst_flat, a=cache_arr[kind], i=slot:
                  s.dma_start(out=d, in_=a[i]), shw)
            slot_spill[sid] = (shw.h, shw.n)
            return (shw.h, shw.n)

    def pe_wait(mark):
        if mark is not None:
            P.wait_mark("tensor", mark)

    ln_state = {"xln_free": None, "ytmb_free": None, "pstr_free": None}

    def emit_ln(src, tt):
        pe_wait(psd_free[0]); pe_wait(psd_free[1])
        pe_wait(psgu_free[0]); pe_wait(psgu_free[1])
        P.wait("vector", pem)
        for s in range(NSUB):
            r0 = tt * TT + s * 128
            if ln_state["xln_free"] is not None:
                P.wait_mark("sync", ln_state["xln_free"])
            P.dma("sync", lambda e, r0=r0: e.dma_start(out=xln[:, :], in_=src[r0:r0 + 128, :]), q_x)
            P.wait("vector", q_x)
            for i in range(4):
                P.emit("vector", lambda v, i=i: v.bn_stats(stats[:, i * 6:(i + 1) * 6],
                                                           xln[:, i * 512:(i + 1) * 512]),
                       chain=(i == 0))
            P.emit("vector", lambda v: v.bn_aggr(mv[:, :], stats[:, 0:24]))
            P.wait_mark("scalar", P.mark("vector"))
            P.emit("scalar", lambda sc: sc.activation(out=rstd[:, :], in_=mv[:, 1:2],
                                                      func=AF.Sqrt, bias=cpool[:, 0:1]))
            P.wait_mark("vector", P.mark("scalar"))
            P.emit("vector", lambda v: v.reciprocal(rstd[:, :], rstd[:, :]))
            if ln_state["ytmb_free"] is not None:
                P.wait_mark("vector", ln_state["ytmb_free"])
            P.emit("vector", lambda v: v.tensor_scalar(ytmb[:, :], xln[:, :], mv[:, 0:1],
                                                       rstd[:, 0:1], ALU.subtract, ALU.mult))
            ln_state["xln_free"] = P.mark("vector")
            pe_wait(P.mark("vector"))
            if ln_state["pstr_free"] is not None:
                pe_wait(ln_state["pstr_free"])
            def trans(t, s=s):
                r = None
                for kc in range(NKC):
                    r = tpose(t, pstr[:, kc * 128:(kc + 1) * 128].rearrange("p (a) -> p a"),
                                    ytmb[:, kc * 128:(kc + 1) * 128], cI16[:, :])
                return r
            P.emit("tensor", trans, pem, 1)
            ln_state["ytmb_free"] = (pem.h, pem.n)
            P.wait("vector", pem)
            P.emit("vector", lambda v, s=s: v.tensor_copy(
                yfm[:, :, s * 128:(s + 1) * 128],
                pstr[:, 0:2048].rearrange("p (kc f) -> p kc f", kc=NKC)))
            ln_state["pstr_free"] = P.mark("vector")

    psgu_free = [None, None]
    psd_free = [None, None]
    upw_free = [None, None, None, None]
    wdh_free = [None] * 6
    wbsb_free = [None]

    def emit_ffn(src, dst, tt):
        parts = dev_parts or ("ln", "router", "expert", "epi")
        emit_ln(src, tt)
        yfm_ready = P.mark("vector")
        if "router" not in parts:
            return

        pe_wait(yfm_ready)
        pe_wait(psd_free[0]); pe_wait(psd_free[1])
        pe_wait(psgu_free[0]); pe_wait(psgu_free[1])
        def zmm(t):
            r = None
            for kc in range(NKC):
                for half in range(2):
                    r = t.matmul(psz[:, half * 512:(half + 1) * 512], wrsb[:, kc, :],
                                 yfm[:, kc, half * 512:(half + 1) * 512],
                                 start=(kc == 0), stop=(kc == NKC - 1))
            return r
        P.emit("tensor", zmm, pem, 1)
        P.wait("vector", pem)
        P.emit("vector", lambda v: v.tensor_copy(zfm[:, :], psz))
        zfm_ready = P.mark("vector")
        for s in range(NSUB):
            pe_wait(P.mark("vector"))
            P.emit("tensor", lambda t, s=s: tpose(t, pszt, zfm[:, s * 128:(s + 1) * 128],
                                                        cIf[0:NE, 0:NE]), pem, 1)
            P.wait("vector", pem)
            P.emit("vector", lambda v: v.tensor_copy(ztm[:, :], pszt))
            P.emit("vector", lambda v: v.max(top8[:, :], ztm[:, :]))
            P.emit("vector", lambda v: v.tensor_scalar(nm1[:, :], top8[:, 0:1], -1.0, None,
                                                       ALU.mult))
            P.wait_mark("scalar", P.mark("vector"))
            P.emit("scalar", lambda sc: sc.activation(out=wtm[:, :], in_=ztm[:, :], func=AF.Exp,
                                                      bias=nm1[:, 0:1], accum_out=sume[:, :]))
            P.wait_mark("vector", P.mark("scalar"))
            P.emit("vector", lambda v: v.reciprocal(sume[:, :], sume[:, :]))
            P.emit("vector", lambda v: v.tensor_scalar(rmask[:, :], ztm[:, :], top8[:, 1:2],
                                                       None, ALU.is_ge))
            P.emit("vector", lambda v: v.scalar_tensor_tensor(wtm[:, :], wtm[:, :], sume[:, 0:1],
                                                              rmask[:, :], ALU.mult, ALU.mult))
            pe_wait(P.mark("vector"))
            P.emit("tensor", lambda t: tpose(t, pswt, wtm[:, :], cIf[:, :]), pem, 1)
            P.wait("vector", pem)
            P.emit("vector", lambda v, s=s: v.tensor_copy(wfm[:, s * 128:(s + 1) * 128], pswt))
        wfm_ready = P.mark("vector")
        psgu_free[0] = wfm_ready
        psgu_free[1] = wfm_ready
        P.wait_mark("scalar", wfm_ready)
        P.wait("scalar", s_wb)
        P.dma("scalar", lambda e: e.dma_start(out=w_scr[:, :], in_=wfm[:, :]), q_wsp)
        if "expert" not in parts:
            return

        after_down = None
        up_done = None
        npe = NPE
        for pp in (parts or ()):
            if isinstance(pp, str) and pp.startswith("npe="):
                npe = int(pp[4:])
        for pe in range(npe):
            routed = pe >= 2
            if routed:
                e = pe - 2
                gkeys = [("wg", e, fc) for fc in range(NF)]
                ukeys = [("wu", e, fc) for fc in range(NF)]
                dkeys = [("wd", e, fc) for fc in range(NF)]
                gsrc = lambda fc, e=e: pk_wg[e, fc]
                usrc = lambda fc, e=e: pk_wu[e, fc]
                dsrc = lambda fc, e=e: pk_wd[e, fc]
            else:
                base = pe * NF
                gkeys = [("wsg", 0, base + fc) for fc in range(NF)]
                ukeys = [("wsu", 0, base + fc) for fc in range(NF)]
                dkeys = [("wsd", 0, base + fc) for fc in range(NF)]
                gsrc = lambda fc, base=base: pk_wsg[base + fc]
                usrc = lambda fc, base=base: pk_wsu[base + fc]
                dsrc = lambda fc, base=base: pk_wsd[base + fc]

            if routed:
                P.wait("sync", q_wsp)
                P.wait_mark("sync", wbsb_free[0])
                P.dma("sync", lambda e2, e=e: e2.dma_start(
                    out=wbsb[:, :],
                    in_=w_scr[e].rearrange("(t) -> t")[None, :].partition_broadcast(128)[:, 0, :]),
                    s_wb)

            for fc in range(NF):
                gb = fc % 2
                ub = 2 + fc % 2
                wg_rdy = load_block(gkeys[fc], "up", upw[gb][:, :, :].rearrange("p a b -> p (a b)"),
                                    lambda g, d, fn=gsrc, fc=fc: g.dma_start(
                                        out=d.rearrange("p (a b) -> p a b", a=NKC),
                                        in_=fn(fc).rearrange("(kc kp) f -> kp kc f", kp=128)),
                                    slot_sem=supw[gb], free_mark=upw_free[gb])
                wu_rdy = load_block(ukeys[fc], "up", upw[ub][:, :, :].rearrange("p a b -> p (a b)"),
                                    lambda g, d, fn=usrc, fc=fc: g.dma_start(
                                        out=d.rearrange("p (a b) -> p a b", a=NKC),
                                        in_=fn(fc).rearrange("(kc kp) f -> kp kc f", kp=128)),
                                    slot_sem=supw[ub], free_mark=upw_free[ub])
                P.wait_mark("tensor", wg_rdy)
                P.wait_mark("tensor", wu_rdy)
                pe_wait(psgu_free[fc % 2])
                if fc == 0:
                    pe_wait(after_down)
                def gumm(t, fc=fc, gb=gb, ub=ub):
                    r = None
                    for kc in range(NKC):
                        for half in range(2):
                            r = t.matmul(psg[fc % 2][:, half * 512:(half + 1) * 512],
                                         upw[gb][:, kc, :],
                                         yfm[:, kc, half * 512:(half + 1) * 512],
                                         start=(kc == 0), stop=(kc == NKC - 1))
                            r = t.matmul(psu[fc % 2][:, half * 512:(half + 1) * 512],
                                         upw[ub][:, kc, :],
                                         yfm[:, kc, half * 512:(half + 1) * 512],
                                         start=(kc == 0), stop=(kc == NKC - 1))
                    return r
                P.emit("tensor", gumm, pem, 1)
                upw_free[gb] = (pem.h, pem.n)
                upw_free[ub] = (pem.h, pem.n)
                P.wait("scalar", pem)
                if P.mark("vector") is not None:
                    P.wait_mark("scalar", P.mark("vector"))
                P.emit("scalar", lambda sc, fc=fc: sc.activation(out=sgscr[:, :],
                                                                 in_=psg[fc % 2], func=AF.Silu))
                P.wait_mark("vector", P.mark("scalar"))
                if routed:
                    if fc == 0:
                        P.wait("vector", s_wb)
                    P.emit("vector", lambda v, fc=fc: v.tensor_mul(sgscr[:, :], sgscr[:, :],
                                                                   psu[fc % 2]))
                    P.emit("vector", lambda v, fc=fc: v.tensor_mul(hreg[:, fc, :], sgscr[:, :],
                                                                   wbsb[:, :]))
                else:
                    P.emit("vector", lambda v, fc=fc: v.tensor_mul(hreg[:, fc, :], sgscr[:, :],
                                                                   psu[fc % 2]))
                psgu_free[fc % 2] = P.mark("vector")
            up_done = P.mark("vector")

            for hf, fcl in enumerate((list(range(6)), list(range(6, NF)))):
                rdys = []
                for j, fc in enumerate(fcl):
                    rdy = load_block(dkeys[fc], "dn", wdh[j],
                                     lambda g, d, fn=dsrc, fc=fc: g.dma_start(out=d, in_=fn(fc)),
                                     slot_sem=swdh[j], free_mark=wdh_free[j])
                    rdys.append(rdy)
                for s in range(NSUB):
                    if s == 0:
                        for rdy in rdys:
                            P.wait_mark("tensor", rdy)
                        pe_wait(up_done)
                        if hf == 0 and pe == 0:
                            pe_wait(psgu_free[0]); pe_wait(psgu_free[1])
                    pe_wait(psd_free[s % 2])
                    def dmm(t, s=s, fcl=fcl):
                        r = None
                        for i, fc in enumerate(fcl):
                            for n in range(4):
                                r = t.matmul(psd[s % 2][:, n * 512:(n + 1) * 512],
                                             hreg[:, fc, s * 128:(s + 1) * 128],
                                             wdh[i][:, n * 512:(n + 1) * 512],
                                             start=(i == 0), stop=(i == len(fcl) - 1))
                        return r
                    P.emit("tensor", dmm, pem, 1)
                    if s == NSUB - 1:
                        for j in range(len(fcl)):
                            wdh_free[j] = (pem.h, pem.n)
                    P.wait("vector", pem)
                    if pe == 0 and hf == 0:
                        P.emit("vector", lambda v, s=s: v.tensor_copy(racc[:, s, :], psd[s % 2]))
                    else:
                        P.emit("vector", lambda v, s=s: v.tensor_add(racc[:, s, :], psd[s % 2],
                                                                     racc[:, s, :]))
                    psd_free[s % 2] = P.mark("vector")
            after_down = P.mark("vector")
            if routed:
                wbsb_free[0] = up_done

        if "epi" not in parts:
            return
        for s in range(NSUB):
            r0 = tt * TT + s * 128
            P.wait_mark("sync", ln_state["xln_free"])
            P.dma("sync", lambda e, r0=r0: e.dma_start(out=xln[:, :], in_=src[r0:r0 + 128, :]), q_x)
            P.wait("vector", q_x)
            P.emit("vector", lambda v, s=s: v.scalar_tensor_tensor(racc[:, s, :], racc[:, s, :],
                                                                   0.5, xln[:, :],
                                                                   ALU.mult, ALU.add))
            ln_state["xln_free"] = P.mark("vector")
            P.wait_mark("scalar", P.mark("vector"))
            P.dma("scalar", lambda e, s=s, r0=r0: e.dma_start(out=dst[r0:r0 + 128, :],
                                                              in_=racc[:, s, :]), q_out)

    miw_free = [None, None]
    opb_free = [None, None]
    s_rep = [dsem("s_rep0"), dsem("s_rep1")]
    s_ea = dsem("s_ea")
    q_acs = dsem("q_acs")
    q_sg = dsem("q_sg")
    q_ys = dsem("q_ys")
    s_ysr = [dsem("s_ysr0"), dsem("s_ysr1")]
    s_sgu = [dsem("s_sgu0"), dsem("s_sgu1")]
    q_xe = dsem("q_xe")
    s_wb = dsem("s_wb")
    q_wsp = dsem("q_wsp")
    mixm = {}

    def emit_mixer(src, dst, tt):
        mparts = dev_parts or ("inx", "dtp", "gate", "conv", "ssd", "gating", "op")
        emit_ln(src, tt)

        if "inx" not in mparts:
            return
        P.wait("vector", q_out)
        P.emit("vector", lambda v: v.tensor_copy(hwork[:, 1:35, 0:3], carry[:, :, :]))
        carry_pre = P.mark("vector")
        px = psg
        px_free = [ln_state["pstr_free"], ln_state["pstr_free"]]
        for oc in range(NCONV):
            bi = oc % 2
            rdy = load_block(("inx", 0, oc), "up", miw[bi][:, :, :].rearrange("p a b -> p (a b)"),
                             lambda g, d, oc=oc: g.dma_start(
                                 out=d.rearrange("p (a b) -> p a b", a=NKC),
                                 in_=pk_inx[oc].rearrange("(kc kp) f -> kp kc f", kp=128)),
                             slot_sem=smiw[bi], free_mark=miw_free[bi])
            P.wait_mark("tensor", rdy)
            pe_wait(px_free[bi])
            def xmm(t, bi=bi):
                r = None
                for kc in range(NKC):
                    for half in range(2):
                        r = t.matmul(px[bi][:, half * 512:(half + 1) * 512], miw[bi][:, kc, :],
                                     yfm[:, kc, half * 512:(half + 1) * 512],
                                     start=(kc == 0), stop=(kc == NKC - 1))
                return r
            P.emit("tensor", xmm, pem, 1)
            miw_free[bi] = (pem.h, pem.n)
            P.wait("vector", pem)
            if oc < 2:
                P.wait_mark("vector", carry_pre)
            P.emit("vector", lambda v, oc=oc, bi=bi: v.tensor_copy(hwork[:, oc + 1, 3:1027],
                                                                   px[bi]))
            px_free[bi] = P.mark("vector")
        if "dtp" not in mparts:
            return
        dtw = miw[0][:, :, :].rearrange("p a b -> p (a b)")[:, 0:1024].rearrange(
            "p (a b) -> p a b", a=NKC)
        rdy = load_block(("indt", 0, 0), "dt",
                         miw[0][:, :, :].rearrange("p a b -> p (a b)")[:, 0:1024],
                         lambda g, d: g.dma_start(
                             out=d.rearrange("p (a b) -> p a b", a=NKC),
                             in_=pk_indt.rearrange("(kc kp) h -> kp kc h", kp=128)),
                         slot_sem=smiw[0], free_mark=miw_free[0])
        P.wait_mark("tensor", rdy)
        pe_wait(px_free[1])
        def dtmm(t):
            r = None
            for kc in range(NKC):
                for half in range(2):
                    r = t.matmul(psdt[:, half * 512:(half + 1) * 512], dtw[:, kc, :],
                                 yfm[:, kc, half * 512:(half + 1) * 512],
                                 start=(kc == 0), stop=(kc == NKC - 1))
            return r
        P.emit("tensor", dtmm, pem, 1)
        miw_free[0] = (pem.h, pem.n)
        P.wait("vector", pem)
        P.emit("vector", lambda v: v.tensor_copy(dtraw[:, :], psdt))
        psdt_done = P.mark("vector")
        px_free[1] = psdt_done

        if "gate" not in mparts:
            return
        for oc in range(NGATE):
            bi = oc % 2
            rdy = load_block(("ing", 0, oc), "up", miw[bi][:, :, :].rearrange("p a b -> p (a b)"),
                             lambda g, d, oc=oc: g.dma_start(
                                 out=d.rearrange("p (a b) -> p a b", a=NKC),
                                 in_=pk_ing[oc].rearrange("(kc kp) f -> kp kc f", kp=128)),
                             slot_sem=smiw[bi], free_mark=miw_free[bi])
            P.wait_mark("tensor", rdy)
            pe_wait(px_free[bi])
            def gmm(t, bi=bi):
                r = None
                for kc in range(NKC):
                    for half in range(2):
                        r = t.matmul(px[bi][:, half * 512:(half + 1) * 512], miw[bi][:, kc, :],
                                     yfm[:, kc, half * 512:(half + 1) * 512],
                                     start=(kc == 0), stop=(kc == NKC - 1))
                return r
            P.emit("tensor", gmm, pem, 1)
            miw_free[bi] = (pem.h, pem.n)
            P.wait("scalar", pem)
            P.wait("scalar", q_sg)
            P.emit("scalar", lambda sc, oc=oc, bi=bi: sc.activation(
                out=ytmb[:, (oc % 2) * 1024:(oc % 2 + 1) * 1024],
                in_=px[bi], func=AF.Silu))
            px_free[bi] = P.mark("scalar")
            P.wait_mark("scalar", P.mark("scalar"))
            P.dma("scalar", lambda e, oc=oc: e.dma_start(
                out=gate_scr[oc], in_=ytmb[:, (oc % 2) * 1024:(oc % 2 + 1) * 1024]),
                q_sg)

        if "conv" not in mparts:
            return
        P.wait_mark("vector", px_free[0]); P.wait_mark("vector", px_free[1])
        P.emit("vector", lambda v: v.tensor_copy(carry[:, :, :], hwork[:, 1:35, 1024:1027]))
        for oc in range(NCONV):
            P.emit("vector", lambda v, oc=oc: v.tensor_scalar(
                convacc[:, :], hwork[:, oc + 1, 0:1024], convw[:, oc, 0:1], None, ALU.mult))
            for kk in range(1, CONVK):
                P.emit("vector", lambda v, oc=oc, kk=kk: v.scalar_tensor_tensor(
                    convacc[:, :], hwork[:, oc + 1, kk:kk + 1024], convw[:, oc, kk:kk + 1],
                    convacc[:, :], ALU.mult, ALU.add))
            P.wait_mark("scalar", P.mark("vector"))
            P.emit("scalar", lambda sc, oc=oc: sc.activation(
                out=hwork[:, oc, 0:1024], in_=convacc[:, :], func=AF.Silu,
                bias=convb[:, oc, 0:1]))
            if oc + 1 < NCONV:
                P.wait_mark("vector", P.mark("scalar"))
        conv_done = P.mark("scalar")

        if "ssd" not in mparts:
            return
        nchunk = NSUB
        for pp in mparts:
            if isinstance(pp, str) and pp.startswith("nch="):
                nchunk = int(pp[4:])
        if tt == 0:
            P.wait("vector", pem)
            P.emit("vector", lambda v: v.memset(Rst[:, :], 0.0))
        for k in range(nchunk):
            cols = slice(k * 128, (k + 1) * 128)
            P.wait_mark("scalar", psdt_done)
            P.emit("scalar", lambda sc, cols=cols: sc.activation(
                out=spt[:, :], in_=dtraw[:, cols], func=AF.Abs, bias=dtb[:, 0:1]))
            P.emit("scalar", lambda sc: sc.activation(out=spt[:, :], in_=spt[:, :],
                                                      func=AF.Exp, scale=-1.0))
            P.emit("scalar", lambda sc: sc.activation(out=spt[:, :], in_=spt[:, :],
                                                      func=AF.Ln, bias=cpool[0:NH, 1:2]))
            P.emit("scalar", lambda sc, cols=cols: sc.activation(
                out=sdt[:, :], in_=dtraw[:, cols], func=AF.Relu, bias=dtb[:, 0:1]))
            P.wait_mark("vector", P.mark("scalar"))
            P.emit("vector", lambda v: v.tensor_add(sdt[:, :], sdt[:, :], spt[:, :]))
            P.emit("vector", lambda v: v.tensor_scalar(acs[:, :], sdt[:, :], negA[:, 0:1],
                                                       None, ALU.mult))
            bufs = [acs, acs2]
            for i, sh in enumerate((1, 2, 4, 8, 16, 32, 64)):
                a, b = bufs[i % 2], bufs[(i + 1) % 2]
                P.emit("vector", lambda v, a=a, b=b, sh=sh: v.tensor_copy(b[:, 0:sh], a[:, 0:sh]))
                P.emit("vector", lambda v, a=a, b=b, sh=sh: v.tensor_add(
                    b[:, sh:], a[:, sh:], a[:, 0:CH - sh]))
            acsf = bufs[1]
            P.wait_mark("scalar", P.mark("vector"))
            P.emit("scalar", lambda sc, acsf=acsf: sc.activation(
                out=spt[:, :], in_=acsf[:, :], func=AF.Exp, scale=-1.0,
                bias=acsf[:, 127:128]))
            P.wait_mark("vector", P.mark("scalar"))
            P.emit("vector", lambda v: v.tensor_mul(wst[:, :], spt[:, :], sdt[:, :]))
            wst_mark = P.mark("vector")
            P.wait("scalar", q_acs)
            P.dma("scalar", lambda e, k=k, acsf=acsf: e.dma_start(out=rowscr[k % 2],
                                                                  in_=acsf[:, :]), q_acs)
            P.dma("scalar", lambda e, k=k, acsf=acsf: e.dma_start(
                out=rowlast[k % 2], in_=acsf[:, 127:128], allow_slow_non_contiguous=True), q_acs)
            pe_wait(wst_mark)
            pe_wait(mixm.get("psT_free"))
            P.emit("tensor", lambda t, acsf=acsf: tpose(t, psTa, acsf[:, :],
                                                              cIf[0:NH, 0:NH]))
            P.emit("tensor", lambda t: tpose(t, psTb, sdt[:, :], cIf[0:NH, 0:NH]))
            P.emit("tensor", lambda t: tpose(t, psTc, wst[:, :], cIf[0:NH, 0:NH]), pem, 1)
            P.wait("vector", pem)
            P.emit("vector", lambda v: v.tensor_copy(acsT[:, :], psTa))
            P.emit("vector", lambda v: v.tensor_copy(dtT[:, :], psTb))
            P.emit("vector", lambda v: v.tensor_copy(wstT[:, :], psTc))
            mixm["psT_free"] = P.mark("vector")
            P.wait_mark("scalar", mixm["psT_free"])
            P.emit("scalar", lambda sc: sc.activation(out=eaT[:, :], in_=acsT[:, :],
                                                      func=AF.Exp))
            eaT_mark = P.mark("scalar")
            P.wait("sync", q_acs)
            P.wait_mark("sync", mixm.get("easum_free"))
            P.dma("sync", lambda e, k=k: e.dma_start(
                out=easum[:, :],
                in_=rowlast[k % 2].rearrange("h o -> (h o)")[None, :].partition_broadcast(128)[:, 0, :]),
                s_ea)
            P.wait("scalar", s_ea)
            P.emit("scalar", lambda sc: sc.activation(out=easum[:, :], in_=easum[:, :],
                                                      func=AF.Exp))
            easum_mark = P.mark("scalar")
            P.wait_mark("tensor", conv_done if k == 0 else None)
            pe_wait(psdt_done)
            pe_wait(mixm.get("pstr2_free"))
            for rnd in range(2):
                if rnd > 0:
                    pe_wait(P.mark("vector"))
                def xtr(t, rnd=rnd, cols=cols):
                    r = None
                    for j in range(16):
                        fc = rnd * 16 + j
                        r = tpose(t, pstr2[:, j * 128:(j + 1) * 128],
                                        hwork[:, fc, cols], cI16[:, :])
                    return r
                P.emit("tensor", xtr, pem, 1)
                P.wait("vector", pem)
                P.emit("vector", lambda v, rnd=rnd: v.tensor_copy(
                    xT[:, rnd * 2048:(rnd + 1) * 2048],
                    pstr2[:, 0:2048]))
            pe_wait(P.mark("vector"))
            P.emit("tensor", lambda t, cols=cols: tpose(t, pstr2[:, 0:128],
                                                              hwork[:, 32, cols], cI16[:, :]),
                   pem, 1)
            P.wait("vector", pem)
            P.emit("vector", lambda v: v.tensor_copy(bT[:, :], pstr2[:, 0:128]))
            mixm["pstr2_free"] = P.mark("vector")
            P.emit("vector", lambda v: v.tensor_tensor(
                xdt.rearrange("p (h d) -> p h d", h=NH),
                xT.rearrange("p (h d) -> p h d", h=NH),
                bcast_last(dtT[:, :], HD), ALU.mult))
            xdt_mark = P.mark("vector")
            pe_wait(mixm.get("psG_free"))
            P.emit("tensor", lambda t, cols=cols: t.matmul(psG[:, 0:128], hwork[:, 32, cols],
                                                           hwork[:, 33, cols],
                                                           start=True, stop=True), pem, 1)
            P.wait("vector", pem)
            P.emit("vector", lambda v: v.tensor_mul(gmask[:, :], psG[:, 0:128], ctril[:, :]))
            mixm["psG_free"] = P.mark("vector")
            gmask_mark = mixm["psG_free"]

            P.wait("vector", q_ys)
            for g in range(8):
                gb = g % 2
                P.wait("sync", q_acs)
                P.wait_mark("sync", mixm.get(f"rep_free{gb}"))
                P.dma("sync", lambda e, k=k, g=g: e.dma_start(
                    out=rep[g % 2][:, :],
                    in_=rowscr[k % 2].rearrange("h t -> (h t)")[None, g * 1024:(g + 1) * 1024]
                        .partition_broadcast(128)[:, 0, :]), s_rep[gb])
                P.wait_mark("vector", mixm.get(f"rbg_free{gb}"))
                P.emit("vector", lambda v, g=g, gb=gb: v.tensor_copy(
                    rbg[:, gb * 512:(gb + 1) * 512], Rst[:, g * 512:(g + 1) * 512]))
                rbg_mark = P.mark("vector")
                pe_wait(rbg_mark)
                pe_wait(mixm.get("psYA_free"))
                def yoff(t, cols=cols, g=g, gb=gb):
                    r = None
                    for j in range(8):
                        r = t.matmul(psYA[:, j * 64:(j + 1) * 64], hwork[:, 33, cols],
                                     rbg[:, gb * 512 + j * 64:gb * 512 + (j + 1) * 64],
                                     start=(j == 0), stop=(j == 7))
                    return r
                P.emit("tensor", yoff, pem, 1)
                mixm[f"rbg_free{gb}"] = (pem.h, pem.n)
                P.wait("vector", s_rep[gb])
                P.emit("vector", lambda v, g=g, gb=gb: v.scalar_tensor_tensor(
                    dsb[:, :].rearrange("p (h t) -> p h t", h=8),
                    rep[g % 2][:, :].rearrange("p (h t) -> p h t", h=8), 1.0,
                    bcast_last(acsT[:, g * 8:(g + 1) * 8], CH),
                    ALU.mult, ALU.subtract))
                mixm[f"rep_free{gb}"] = P.mark("vector")
                P.wait_mark("scalar", P.mark("vector"))
                P.wait_mark("scalar", eaT_mark if g == 0 else None)
                P.emit("scalar", lambda sc: sc.activation(out=dsb[:, :], in_=dsb[:, :],
                                                          func=AF.Exp))
                P.wait_mark("vector", P.mark("scalar"))
                P.wait_mark("vector", mixm.get(f"glb_free{gb}"))
                P.emit("vector", lambda v, gb=gb: v.scalar_tensor_tensor(
                    glb[gb][:, :].rearrange("p (h t) -> p h t", h=8),
                    dsb[:, :].rearrange("p (h t) -> p h t", h=8), 1.0,
                    bcast(gmask[:, :], 0, 8), ALU.min, ALU.mult))
                glb_mark = P.mark("vector")
                pe_wait(glb_mark)
                if g == 0:
                    pe_wait(xdt_mark)
                    pe_wait(mixm.get("psYB_free"))
                def ydiag(t, g=g, gb=gb):
                    r = None
                    for j in range(8):
                        r = t.matmul(psYB[:, j * 64:(j + 1) * 64],
                                     glb[gb][:, j * 128:(j + 1) * 128],
                                     xdt[:, (g * 8 + j) * 64:(g * 8 + j + 1) * 64],
                                     start=(j == 0), stop=(j == 7))
                    return r
                P.emit("tensor", ydiag, pem, 1)
                mixm[f"glb_free{gb}"] = (pem.h, pem.n)
                ymm_mark = (pem.h, pem.n)
                P.wait("vector", pem)
                P.emit("vector", lambda v, g=g: v.tensor_tensor(
                    yevt[:, :].rearrange("p (h d) -> p h d", h=8),
                    psYA.rearrange("p (h d) -> p h d", h=8),
                    bcast_last(eaT[:, g * 8:(g + 1) * 8], HD), ALU.mult))
                mixm["psYA_free"] = P.mark("vector")
                P.emit("vector", lambda v: v.scalar_tensor_tensor(
                    yevt[:, :], psYB, 1.0, yevt[:, :], ALU.mult, ALU.add))
                mixm["psYB_free"] = P.mark("vector")
                P.emit("vector", lambda v, g=g: v.tensor_tensor(
                    dres[:, :].rearrange("p (h d) -> p h d", h=8),
                    xT[:, g * 512:(g + 1) * 512].rearrange("p (h d) -> p h d", h=8),
                    bcast_last(dD[:, g * 8:(g + 1) * 8], HD), ALU.mult))
                P.emit("vector", lambda v, g=g: v.tensor_add(
                    yssdb[:, g * 512:(g + 1) * 512], yevt[:, :], dres[:, :]))
            P.wait("vector", pem)
            P.emit("vector", lambda v: v.tensor_tensor(
                xw.rearrange("p (h d) -> p h d", h=NH),
                xT.rearrange("p (h d) -> p h d", h=NH),
                bcast_last(wstT[:, :], HD), ALU.mult))
            xw_mark = P.mark("vector")
            psS_rot = [psS, psYA]
            for g in range(8):
                pe_wait(xw_mark if g == 0 else None)
                pe_wait(mixm.get(f"psS_free{g % 2}"))
                if g == 0:
                    pe_wait(mixm.get("psYA_free"))
                def smm(t, cols=cols, g=g):
                    r = None
                    for j in range(8):
                        r = t.matmul(psS_rot[g % 2][:, j * 64:(j + 1) * 64], bT[:, :],
                                     xw[:, (g * 8 + j) * 64:(g * 8 + j + 1) * 64],
                                     start=(j == 0), stop=(j == 7))
                    return r
                P.emit("tensor", smm, pem, 1)
                P.wait("vector", pem)
                P.wait_mark("vector", easum_mark if g == 0 else None)
                P.emit("vector", lambda v, g=g: v.tensor_tensor(
                    rupd[:, :].rearrange("p (h d) -> p h d", h=8),
                    Rst[:, g * 512:(g + 1) * 512].rearrange("p (h d) -> p h d", h=8),
                    bcast_last(easum[:, g * 8:(g + 1) * 8], HD), ALU.mult))
                P.emit("vector", lambda v, g=g: v.tensor_add(
                    Rst[:, g * 512:(g + 1) * 512], rupd[:, :], psS_rot[g % 2]))
                mixm[f"psS_free{g % 2}"] = P.mark("vector")
            mixm["easum_free"] = P.mark("vector")
            P.wait_mark("scalar", P.mark("vector"))
            P.dma("scalar", lambda e, k=k: e.dma_start(out=yssd_scr[k], in_=yssdb[:, :]), q_ys)

        if "gating" not in mparts:
            return
        phase_barrier()
        for k2 in range(NSUB):
            kb = k2 % 2
            P.wait("sync", q_ys)
            P.wait_mark("sync", mixm.get(f"ysr_free{kb}"))
            P.dma("sync", lambda e, k2=k2, kb=kb: e.dma_start(out=ysr[kb][:, :],
                                                              in_=yssd_scr[k2]), s_ysr[kb])
            P.wait("sync", q_sg)
            P.wait_mark("sync", mixm.get(f"sgu_free{kb}"))
            P.dma("sync", lambda e, k2=k2, kb=kb: e.dma_start(
                out=sgu[kb][:, :, :],
                in_=gate_scr[:, :, k2 * 128:(k2 + 1) * 128].rearrange("oc p t -> p oc t")),
                s_sgu[kb])
            P.wait("tensor", s_sgu[kb])
            pe_wait(mixm.get("sgT_free"))
            for rnd in range(2):
                if rnd > 0:
                    pe_wait(P.mark("vector"))
                def str_(t, rnd=rnd, kb=kb):
                    r = None
                    for j in range(16):
                        fc = rnd * 16 + j
                        r = tpose(t, pstr[:, (rnd * 16 + j) * 128:(rnd * 16 + j + 1) * 128],
                                        sgu[kb][:, fc, :], cI16[:, :])
                    return r
                P.emit("tensor", str_, pem, 1)
            P.wait("vector", pem)
            P.emit("vector", lambda v: v.tensor_copy(sgT[:, :], pstr[:, 0:4096]))
            mixm[f"sgu_free{kb}"] = P.mark("vector")
            mixm["sgT_free"] = P.mark("vector")
            P.wait("vector", s_ysr[kb])
            P.emit("vector", lambda v, kb=kb: v.tensor_mul(yfb[:, :], ysr[kb][:, :], sgT[:, :]))
            mixm[f"ysr_free{kb}"] = P.mark("vector")
            P.wait_mark("scalar", P.mark("vector"))
            P.emit("scalar", lambda sc: sc.activation(out=sqs[:, :], in_=yfb[:, :],
                                                      func=AF.Square, accum_out=s2c[:, :]))
            P.wait_mark("vector", P.mark("scalar"))
            P.emit("vector", lambda v: v.tensor_scalar(s2c[:, :], s2c[:, :], 1.0 / DIN, None,
                                                       ALU.mult))
            P.wait_mark("scalar", P.mark("vector"))
            P.emit("scalar", lambda sc: sc.activation(out=s2c[:, :], in_=s2c[:, :],
                                                      func=AF.Sqrt, bias=cpool[:, 0:1]))
            P.wait_mark("vector", P.mark("scalar"))
            P.emit("vector", lambda v: v.reciprocal(rstd2[:, :], s2c[:, :]))
            P.emit("vector", lambda v: v.tensor_scalar(yfb[:, :], yfb[:, :], rstd2[:, 0:1],
                                                       None, ALU.mult))
            yn_mark = P.mark("vector")
            pe_wait(yn_mark)
            pe_wait(mixm.get("pstr2_free"))
            for rnd in range(2):
                if rnd > 0:
                    pe_wait(P.mark("vector"))
                def ytr(t, rnd=rnd):
                    r = None
                    for j in range(16):
                        r = tpose(t, pstr2[:, j * 128:(j + 1) * 128],
                                        yfb[:, (rnd * 16 + j) * 128:(rnd * 16 + j + 1) * 128],
                                        cI16[:, :])
                    return r
                P.emit("tensor", ytr, pem, 1)
                P.wait("vector", pem)
                P.emit("vector", lambda v, rnd=rnd, k2=k2: v.tensor_copy(
                    ynfm[:, rnd * 16:(rnd + 1) * 16, k2 * 128:(k2 + 1) * 128],
                    pstr2[:, 0:2048].rearrange("p (fc f) -> p fc f", fc=16)))
            mixm["pstr2_free"] = P.mark("vector")

        if "op" not in mparts:
            return
        phase_barrier()
        for s2 in range(4):
            for di in range(NGATE):
                bi = di % 2
                rdy = load_block(("op", 0, di), "dn", opbuf[bi],
                                 lambda g, d, di=di: g.dma_start(out=d, in_=pk_op[di]),
                                 slot_sem=sopb[bi],
                                 postproc=(lambda bi=bi, di=di: P.emit(
                                     "vector", lambda v: v.tensor_scalar(
                                         opbuf[bi][:, :], opbuf[bi][:, :], nwfm[:, di:di + 1],
                                         None, ALU.mult))),
                                 free_mark=opb_free[bi])
                P.wait_mark("tensor", rdy)
                if di < 2:
                    pe_wait(psd_free[0]); pe_wait(psd_free[1])
                def opmm(t, s2=s2, di=di, bi=bi):
                    r = None
                    for ss in range(2):
                        sub = 2 * s2 + ss
                        for n in range(4):
                            r = t.matmul(psd[ss][:, n * 512:(n + 1) * 512],
                                         ynfm[:, di, sub * 128:(sub + 1) * 128],
                                         opbuf[bi][:, n * 512:(n + 1) * 512],
                                         start=(di == 0), stop=(di == NGATE - 1))
                    return r
                P.emit("tensor", opmm, pem, 1)
                opb_free[bi] = (pem.h, pem.n)
            for ss in range(2):
                sub = 2 * s2 + ss
                r0 = tt * TT + sub * 128
                P.wait("sync", q_out)
                P.wait_mark("sync", mixm.get("xep_free"))
                P.dma("sync", lambda e, r0=r0: e.dma_start(out=xep2[:, :],
                                                           in_=src[r0:r0 + 128, :]), q_xe)
                P.wait("vector", q_xe)
                P.wait("vector", pem)
                P.emit("vector", lambda v, ss=ss: v.tensor_add(xep2[:, :], psd[ss], xep2[:, :]))
                psd_free[ss] = P.mark("vector")
                mixm["xep_free"] = P.mark("vector")
                P.wait_mark("scalar", P.mark("vector"))
                P.dma("scalar", lambda e, r0=r0: e.dma_start(out=dst[r0:r0 + 128, :],
                                                             in_=xep2[:, :]), q_out)

    P.dma("sync", lambda e: e.dma_start(out=cIf[:, :], in_=idf_d[:, :]), q_x)
    P.dma("sync", lambda e: e.dma_start(out=ctril[:, :], in_=tril_d[:, :]), q_x)
    P.dma("sync", lambda e: e.dma_start(out=oh8[:, :], in_=oh8_d[:, :]), q_x)
    P.dma("sync", lambda e: e.dma_start(out=convw[:, :, :],
                                        in_=convw_d.rearrange("(fc p) k -> p fc k", p=128),
                                        allow_slow_non_contiguous=True), q_x)
    P.dma("sync", lambda e: e.dma_start(out=convb[:, :, :],
                                        in_=convb_d.rearrange("(fc p) o -> p fc o", p=128),
                                        allow_slow_non_contiguous=True), q_x)
    P.dma("sync", lambda e: e.dma_start(out=dtb[:, :], in_=dtb_d[:, :]), q_x)
    P.dma("sync", lambda e: e.dma_start(
        out=dD[:, :], in_=dvec_d.rearrange("o h -> (o h)")[None, :].partition_broadcast(128)[:, 0, :]), q_x)
    P.dma("sync", lambda e: e.dma_start(
        out=nwfm, in_=dataclasses.replace(nw_d, ap=[[1, 128], [128, NGATE]]),
        allow_slow_non_contiguous=True), q_x)
    P.dma("sync", lambda e: e.dma_start(out=negA[:, :], in_=alog_d[:, :]), q_x)
    P.dma("gpsimd", lambda g: g.dma_start(out=cI16[:, :], in_=id16_d[:, :]), q_w)
    P.dma("gpsimd", lambda g: g.dma_start(
        out=wrsb[:, :, :], in_=pk_wr.rearrange("(kc kp) e -> kp kc e", kp=128)), q_w)
    P.emit("vector", lambda v: v.memset(cpool[:, 0:1], 1e-5), chain=False)
    P.emit("vector", lambda v: v.memset(cpool[:, 1:2], 1.0))
    P.emit("vector", lambda v: v.memset(carry[:, :, :], 0.0))
    P.wait("scalar", q_x)
    P.emit("scalar", lambda sc: sc.activation(out=negA[:, :], in_=negA[:, :], func=AF.Exp))
    P.wait_mark("vector", P.mark("scalar"))
    P.emit("vector", lambda v: v.tensor_scalar(negA[:, :], negA[:, :], -1.0, None, ALU.mult))
    phase_barrier()

    srcs = {"ffn1": (x_in, xmid1), "mixer": (xmid1, xmid2), "ffn2": (xmid2, out_d)}
    srcs[phases[-1]] = (srcs[phases[-1]][0], out_d)
    for tt in tiles:
        for ph in phases:
            s_, d_ = srcs[ph]
            if ph == "mixer":
                emit_mixer(s_, d_, tt)
            else:
                emit_ffn(s_, d_, tt)
            phase_barrier()

    P.wait("scalar", q_out)

    with nc.Block() as block:
        P.run(block)
    return nc, es



def _pack(inputs):
    f32 = np.float32
    c = np.ascontiguousarray
    ipw = np.asarray(inputs["L0_attn_in_proj_w"], f32)
    Wg = np.asarray(inputs["L0_ffn_Wg"], f32)
    Wu = np.asarray(inputs["L0_ffn_Wu"], f32)
    Wd = np.asarray(inputs["L0_ffn_Wd"], f32)
    Wsg = np.asarray(inputs["L0_ffn_Wsg"], f32)
    Wsu = np.asarray(inputs["L0_ffn_Wsu"], f32)
    Wsd = np.asarray(inputs["L0_ffn_Wsd"], f32)
    Wr = np.asarray(inputs["L0_ffn_Wr"], f32)
    opw = np.asarray(inputs["L0_attn_out_proj_w"], f32)
    oh8 = np.zeros((NE, NE * 128), f32)
    for e in range(NE):
        oh8[e, e * 128:(e + 1) * 128] = 1.0
    d = {
        "pk_wsg": c(Wsg.T.reshape(D, 22, 128).transpose(1, 0, 2)),
        "pk_wsu": c(Wsu.T.reshape(D, 22, 128).transpose(1, 0, 2)),
        "pk_wg": c(Wg.transpose(0, 2, 1).reshape(NE, D, NF, 128).transpose(0, 2, 1, 3)),
        "pk_wu": c(Wu.transpose(0, 2, 1).reshape(NE, D, NF, 128).transpose(0, 2, 1, 3)),
        "pk_wsd": c(Wsd.T.reshape(22, 128, D)),
        "pk_wd": c(Wd.transpose(0, 2, 1).reshape(NE, NF, 128, D)),
        "pk_wr": c(Wr.T),
        "pk_inx": c(ipw[DIN:DIN + CONVD].T.reshape(D, NCONV, 128).transpose(1, 0, 2)),
        "pk_ing": c(ipw[0:DIN].T.reshape(D, NGATE, 128).transpose(1, 0, 2)),
        "pk_indt": c(ipw[DIN + CONVD:].T),
        "pk_op": c(opw.T.reshape(NGATE, 128, D)),
        "convw": np.asarray(inputs["L0_attn_conv_w"], f32),
        "convb": np.asarray(inputs["L0_attn_conv_b"], f32).reshape(CONVD, 1),
        "dtb": np.asarray(inputs["L0_attn_dt_bias"], f32).reshape(NH, 1),
        "alog": np.asarray(inputs["L0_attn_A_log"], f32).reshape(NH, 1),
        "dvec": np.asarray(inputs["L0_attn_D"], f32).reshape(1, NH),
        "nw": np.asarray(inputs["L0_attn_norm_w"], f32).reshape(1, DIN),
        "id16": np.eye(128, dtype=ml_dtypes.bfloat16),
        "idf": np.eye(128, dtype=f32),
        "tril": np.triu(np.ones((128, 128), f32)),
        "oh8": oh8,
    }
    return d


_prog_cache = {}

def kernel(**inputs):
    key = "full"
    if key not in _prog_cache:
        _prog_cache[key] = build_program()
    nc, es = _prog_cache[key]
    shared = _pack(inputs)
    x = np.asarray(inputs["L0_attn_hidden_states"], np.float32)
    in_maps = []
    for i in range(B):
        m = dict(shared)
        m["x"] = np.ascontiguousarray(x[i])
        in_maps.append(m)
    res = run_bass_kernel_spmd(nc, in_maps, core_ids=list(range(B)))
    return np.stack([res.results[i]["out"] for i in range(B)], axis=0)


# version 02
# speedup vs baseline: 1.0850x; 1.0850x over previous
import dataclasses
import numpy as np
import ml_dtypes
from contextlib import ExitStack

import concourse.bass as bass
import concourse.mybir as mybir
from concourse.bass_utils import run_bass_kernel_spmd

F32 = mybir.dt.float32
BF16 = mybir.dt.bfloat16
AF = mybir.ActivationFunctionType
ALU = mybir.AluOpType
AX = mybir.AxisListType

B = 8
L = 4096
D = 2048
NKC = 16
DIN = 4096
NH, HD, DST = 64, 64, 128
CONVD = 4352
NCONV = 34
NGATE = 32
CONVK = 4
CH = 128
F_E = 1408
NF = 11
FS = 2816
NE = 8
NPE = 10
TT = 1024
NTT = L // TT
NSUB = TT // 128


class Sem:
    def __init__(self, h, name=""):
        self.h = h; self.n = 0; self.name = name

CHAINED = ("gpsimd", "vector", "scalar")

class Plan:
    ENGINES = ("sync", "gpsimd", "tensor", "vector", "scalar")

    def __init__(self, mksem):
        self.q = {e: [] for e in self.ENGINES}
        self.chain = {e: Sem(mksem(f"chain_{e}"), f"chain_{e}") for e in CHAINED}
        self.last = {}
        self.n_instr = 0

    def emit(self, e, fn, inc=None, k=1, chain=True):
        q = self.q[e]; self.n_instr += 1
        if e in CHAINED:
            if chain and e in self.last:
                h, v = self.last[e]
                q.append(lambda eng, h=h, v=v: eng.wait_ge(h, v))
            if inc is None:
                s = self.chain[e]; s.n += 1; kk = 1
            else:
                s = inc; s.n += k; kk = k
            q.append(lambda eng, fn=fn, h=s.h, kk=kk: fn(eng).then_inc(h, kk))
            self.last[e] = (s.h, s.n)
        else:
            if inc is not None:
                inc.n += k
                q.append(lambda eng, fn=fn, h=inc.h, kk=k: fn(eng).then_inc(h, kk))
            else:
                q.append(fn)

    def dma(self, e, fn, sem):
        self.n_instr += 1
        sem.n += 16
        self.q[e].append(lambda eng, fn=fn, h=sem.h: fn(eng).then_inc(h, 16))

    def wait(self, e, sem, v=None):
        v = sem.n if v is None else v
        if v > 0:
            self.q[e].append(lambda eng, h=sem.h, v=v: eng.wait_ge(h, v))

    def mark(self, e):
        return self.last.get(e)

    def wait_mark(self, e_waiter, mark):
        if mark is not None:
            h, v = mark
            self.q[e_waiter].append(lambda eng, h=h, v=v: eng.wait_ge(h, v))

    def run(self, block):
        for e in self.ENGINES:
            if self.q[e]:
                getattr(block, e)(lambda eng, ops=tuple(self.q[e]): [f(eng) for f in ops])


def bcast(ap, idx, count):
    l = [list(x) for x in ap.ap]
    l.insert(idx + 1, [0, count])
    return dataclasses.replace(ap, ap=l)


def bcast_last(ap, count):
    l = [list(x) for x in ap.ap] + [[0, count]]
    return dataclasses.replace(ap, ap=l)


class Carver:
    def __init__(self, arena_handle, nbytes):
        self.t = arena_handle
        self.nbytes = nbytes
        self.watermark = 0
        self.layout = {}

    def at(self, off, parts, cols, dt, name=""):
        size = mybir.dt.size(dt)
        assert off % 32 == 0, (name, off)
        assert off + cols * size <= self.nbytes, (name, off, cols, size, self.nbytes)
        self.watermark = max(self.watermark, off + cols * size)
        if name:
            self.layout[name] = (off, parts, cols, str(dt))
        ap = self.t.ap()[0:parts, off:off + cols * size].bitcast(dt)
        return ap

    def at3(self, off, parts, a, b, dt, name=""):
        return self.at(off, parts, a * b, dt, name).rearrange(
            "p (a b) -> p a b", a=a)


def pe_sem_mark(sem):
    return (sem.h, sem.n)


def tpose(t, out, in_, identity):
    return t.matmul(out, in_, identity, start=True, stop=True, is_transpose=True)



def build_program(dev_tiles=None, dev_phases=None, extra_taps=(), dev_parts=None):
    tiles = list(range(NTT)) if dev_tiles is None else list(dev_tiles)
    phases = ("ffn1", "mixer", "ffn2") if dev_phases is None else tuple(dev_phases)

    nc = bass.Bass("TRN2", target_bir_lowering=False)
    es = ExitStack()

    def din(name, shape, dt=F32):
        return nc.dram_tensor(name, list(shape), dt, kind="ExternalInput").ap()

    x_in  = din("x", (L, D))
    pk_wsg = din("pk_wsg", (22, D, 128))
    pk_wsu = din("pk_wsu", (22, D, 128))
    pk_wg  = din("pk_wg", (NE, NF, D, 128))
    pk_wu  = din("pk_wu", (NE, NF, D, 128))
    pk_wsd = din("pk_wsd", (22, 128, D))
    pk_wd  = din("pk_wd", (NE, NF, 128, D))
    pk_wr  = din("pk_wr", (D, NE))
    pk_inx = din("pk_inx", (NCONV, D, 128))
    pk_ing = din("pk_ing", (NGATE, D, 128))
    pk_indt = din("pk_indt", (D, NH))
    pk_op  = din("pk_op", (NGATE, 128, D))
    convw_d = din("convw", (CONVD, CONVK))
    convb_d = din("convb", (CONVD, 1))
    dtb_d  = din("dtb", (NH, 1))
    alog_d = din("alog", (NH, 1))
    dvec_d = din("dvec", (1, NH))
    nw_d   = din("nw", (1, DIN))
    id16_d = din("id16", (128, 128), BF16)
    idf_d  = din("idf", (128, 128))
    tril_d = din("tril", (128, 128))
    oh8_d  = din("oh8", (NE, NE * 128))

    out_d = nc.dram_tensor("out", [L, D], F32, kind="ExternalOutput").ap()

    def dscr(name, shape, dt=F32):
        return nc.dram_tensor(name, list(shape), dt, kind="Internal").ap()

    xmid1 = dscr("xmid1", (L, D))
    w_scr = dscr("w_scr", (NE, TT))
    xmid2 = dscr("xmid2", (L, D))
    rowscr = dscr("rowscr", (2, NH, CH))
    rowlast = dscr("rowlast", (2, NH, 1))
    gate_scr = dscr("gate_scr", (NGATE, 128, TT), BF16)
    yssd_scr = dscr("yssd_scr", (NSUB, 128, DIN), BF16)
    N_UP = 22 + 22 + NE * NF + NE * NF + NCONV + NGATE
    cache_up = dscr("cache_up", (N_UP, 128, NKC * 128), BF16)
    cache_dt = dscr("cache_dt", (1, 128, NKC * NH), BF16)
    cache_dn = dscr("cache_dn", (22 + NE * NF + NGATE, 128, D), BF16)

    avail = nc.sbuf_bytes_remaining // 32 * 32
    arena = es.enter_context(nc.sbuf_tensor("arena", [128, avail], mybir.dt.uint8))
    C = Carver(arena, avail)

    cur = [0]
    def take(nbytes, align=32):
        cur[0] = (cur[0] + align - 1) // align * align
        off = cur[0]; cur[0] += nbytes
        assert cur[0] <= avail, (off, nbytes, cur[0], avail)
        return off

    cI16  = C.at(take(256), 128, 128, BF16, "cI16")
    cIf   = C.at(take(512), 128, 128, F32, "cIf")
    ctril = C.at(take(512), 128, 128, F32, "ctril")
    cpool = C.at(take(32), 128, 8, F32, "cpool")
    convw = C.at3(take(544), 128, NCONV, CONVK, F32, "convw")
    convb = C.at3(take(160), 128, NCONV, 1, F32, "convb")
    dD    = C.at(take(256), 128, NH, F32, "dD")
    wrsb  = C.at3(take(256), 128, NKC, NE, BF16, "wrsb")
    oh8   = C.at(take(4096), NE, NE * 128, F32, "oh8")
    dtb   = C.at(take(32), NH, 1, F32, "dtb")
    negA  = C.at(take(32), NH, 1, F32, "negA")
    carry = C.at3(take(224), 128, NCONV, 3, BF16, "carry")
    yfm   = C.at3(take(32768), 128, NKC, TT, BF16, "yfm")
    xln   = [C.at(take(8192), 128, D, F32, "xln0"),
             C.at(take(8192), 128, D, F32, "xln1")]
    ytmb  = C.at(take(4096), 128, D, BF16, "ytmb")
    stats = C.at(take(96), 128, 24, F32, "stats")
    mv    = C.at(take(32), 128, 2, F32, "mv")
    rstd  = C.at(take(32), 128, 1, F32, "rstd")
    U0 = (cur[0] + 31) // 32 * 32

    cur[0] = U0
    racc  = C.at3(take(65536), 128, NSUB, D, F32, "racc")
    hreg  = C.at3(take(22528), 128, NF, TT, BF16, "hreg")
    wdh   = [C.at(take(4096), 128, D, BF16, f"wdh{i}") for i in range(9)]
    upw   = [C.at3(take(4096), 128, NKC, 128, BF16, f"upw{i}") for i in range(4)]
    off_sg0 = take(4096); off_sg1 = take(4096)
    sgscr = [C.at(off_sg0, 128, TT, F32, "sgscr0"),
             C.at(off_sg1, 128, TT, F32, "sgscr1")]
    zfm   = C.at(off_sg0, NE, TT, F32, "zfm")
    wall  = C.at(take(256), 128, NSUB * NE, F32, "wall")
    ztm   = C.at(take(32), 128, NE, F32, "ztm")
    top8  = C.at(take(32), 128, NE, F32, "top8")
    wtm   = C.at(take(32), 128, NE, F32, "wtm")
    nm1   = C.at(take(32), 128, 1, F32, "nm1")
    sume  = C.at(take(32), 128, 1, F32, "sume")
    rmask = C.at(take(32), 128, NE, F32, "rmask")
    ffn_end = cur[0]

    cur[0] = U0
    HSTR = 1028
    off_hwork = take(2 * 35 * HSTR)
    hwork = C.at3(off_hwork, 128, 35, HSTR, BF16, "hwork")
    Rst   = C.at(take(16384), 128, DIN, F32, "Rst")
    off_yssdb = take(8192); yssdb = C.at(off_yssdb, 128, DIN, BF16, "yssdb")
    off_xT = take(8192);  xT  = C.at(off_xT, 128, DIN, BF16, "xT")
    off_xdt = take(8192); xdt = C.at(off_xdt, 128, DIN, BF16, "xdt")
    xw    = C.at(off_xdt, 128, DIN, BF16, "xw")
    off_rep = [take(4096), take(4096)]
    rep   = [C.at(off_rep[i], 128, 8 * CH, F32, f"rep{i}") for i in range(2)]
    off_dsb = take(4096)
    dsb   = C.at(off_dsb, 128, 8 * CH, F32, "dsb")
    off_glb = [take(2048), take(2048)]
    glb   = [C.at(off_glb[i], 128, 8 * CH, BF16, f"glb{i}") for i in range(2)]
    off_miw = [take(4096), take(4096)]
    miw   = [C.at3(off_miw[i], 128, NKC, 128, BF16, f"miw{i}") for i in range(2)]
    off_acs = take(512);  acs  = C.at(off_acs, NH, CH, F32, "acs")
    acs2  = C.at(take(512), NH, CH, F32, "acs2")
    sdt   = C.at(take(512), NH, CH, F32, "sdt")
    wst   = C.at(take(512), NH, CH, F32, "wst")
    spt   = C.at(take(512), NH, CH, F32, "spt")
    acsT  = C.at(take(256), 128, NH, F32, "acsT")
    dtT   = C.at(take(256), 128, NH, F32, "dtT")
    wstT  = C.at(take(256), 128, NH, F32, "wstT")
    eaT   = C.at(take(256), 128, NH, F32, "eaT")
    easum = C.at(take(256), 128, NH, F32, "easum")
    gmask = C.at(take(512), 128, 128, F32, "gmask")
    off_bT = take(256); bT = C.at(off_bT, 128, 128, BF16, "bT")
    rbg   = C.at(take(2048), 128, 1024, BF16, "rbg")
    yevt  = C.at(take(2048), 128, 512, F32, "yevt")
    off_dres = take(2048)
    dres  = C.at(off_dres, 128, 512, F32, "dres")
    rupd  = C.at(off_dres, 128, 512, F32, "rupd")
    off_convacc = off_rep[0]
    convacc = C.at(off_convacc, 128, TT, F32, "convacc")
    off_dtraw = take(4096); dtraw = C.at(off_dtraw, NH, TT, F32, "dtraw")
    nwfm  = C.at(take(128), 128, NGATE, F32, "nwfm")
    mix_end = cur[0]
    ynfm  = C.at3(off_hwork, 128, NGATE, TT, BF16, "ynfm")
    ysr   = [C.at(off_xT, 128, DIN, BF16, "ysr0"),
             C.at(off_xdt, 128, DIN, BF16, "ysr1")]
    sgu   = [C.at3(off_rep[0], 128, NGATE, 128, BF16, "sgu0"),
             C.at3(off_dsb, 128, NGATE, 128, BF16, "sgu1")]
    sgT   = C.at(off_yssdb, 128, DIN, BF16, "sgT")
    yfb   = C.at(off_miw[0], 128, DIN, BF16, "yfb")
    sqs   = C.at(off_acs, 128, DIN, BF16, "sqs")
    s2c   = C.at(off_dtraw, 128, 1, F32, "s2c")
    rstd2 = C.at(off_dtraw + 32, 128, 1, F32, "rstd2")
    opbuf = [C.at(off_glb[0], 128, D, BF16, "opbuf0"),
             C.at(off_rep[1], 128, D, BF16, "opbuf1")]
    xep2  = C.at(off_yssdb, 128, D, F32, "xep2")
    assert max(ffn_end, mix_end) <= avail, (ffn_end, mix_end, avail)
    build_program.layout = dict(C.layout)

    psum = es.enter_context(nc.psum_tensor("psum", [128, 4096], F32))
    PS = psum.ap()
    PSB = psum.ap().bitcast(BF16)
    psg  = [PS[:, 0:1024], PS[:, 2048:3072]]
    psu  = [PS[:, 1024:2048], PS[:, 3072:4096]]
    psd  = [PS[:, 0:2048], PS[:, 2048:4096]]
    pstr = PSB[:, 0:4096]
    psz  = PS[0:NE, 1024:2048]
    pszt = PS[:, 2048:2048 + NE]
    pswt = PS[0:NE, 2560:2688]
    pswb = PS[:, 3072:4096]
    psx  = [PS[:, 0:1024], PS[:, 1024:2048]]
    psdt = PS[0:NH, 2048:3072]
    psYA = PS[:, 0:512]
    psYB = PS[:, 512:1024]
    psS  = PS[:, 1024:1536]
    psG  = PS[:, 1536:2048]
    pstr2 = PSB[:, 4096:6144]
    psTa = PS[:, 3072:3136]
    psTb = PS[:, 3136:3200]
    psTc = PS[:, 3200:3264]

    def mksem(name):
        return es.enter_context(nc.semaphore(name))
    P = Plan(mksem)
    ALL_DMA_SEMS = []
    def dsem(name):
        s = Sem(mksem(name))
        ALL_DMA_SEMS.append(s)
        return s
    q_x   = dsem("q_x")
    q_xl  = [dsem("q_xl0"), dsem("q_xl1")]
    mkpair = lambda n: (dsem(n + "s"), dsem(n + "h"))
    supw  = [mkpair(f"supw{i}") for i in range(4)]
    swdh  = [mkpair(f"swdh{i}") for i in range(9)]
    smiw  = [mkpair(f"smiw{i}") for i in range(2)]
    sopb  = [mkpair(f"sopb{i}") for i in range(2)]
    q_w   = dsem("q_w")
    q_cr  = dsem("q_cr")
    q_sp  = dsem("q_sp")
    q_rw  = dsem("q_rw")
    q_out = dsem("q_out")
    pem   = Sem(mksem("pem"))

    def phase_barrier():
        marks = [P.mark(e) for e in CHAINED]
        for e in P.ENGINES:
            for s in ALL_DMA_SEMS:
                P.wait(e, s)
            P.wait(e, pem)
            for m in marks:
                P.wait_mark(e, m)

    cache_reg = {}
    cache_next = {"up": 0, "dt": 0, "dn": 0}
    cache_arr = {"up": cache_up, "dt": cache_dt, "dn": cache_dn}
    cache_spill_mark = {}
    slot_spill = {}

    def load_block(key, kind, dst_flat, src_fp32_fn, slot_sem, postproc=None, free_mark=None):
        ssw, shw = slot_sem
        sid = id(ssw)
        if key not in cache_reg:
            P.wait_mark("gpsimd", free_mark)
            P.wait_mark("gpsimd", slot_spill.get(sid))
            slot = cache_next[kind]; cache_next[kind] += 1
            cache_reg[key] = slot
            P.dma("gpsimd", lambda g, d=dst_flat: src_fp32_fn(g, d), ssw)
            ready = (ssw.h, ssw.n)
            if postproc is not None:
                P.wait("vector", Sem(ssw.h), ssw.n)
                postproc()
                ready = P.mark("vector")
            if postproc is not None:
                P.wait_mark("scalar", ready)
            else:
                P.wait("scalar", Sem(ssw.h), ssw.n)
            P.dma("scalar", lambda s, d=dst_flat, a=cache_arr[kind], i=slot:
                  s.dma_start(out=a[i], in_=d), shw)
            cache_spill_mark[key] = (shw.h, shw.n)
            slot_spill[sid] = (shw.h, shw.n)
            return ready
        else:
            slot = cache_reg[key]
            P.wait_mark("sync", free_mark)
            P.wait_mark("sync", slot_spill.get(sid))
            P.wait_mark("sync", cache_spill_mark[key])
            P.dma("sync", lambda s, d=dst_flat, a=cache_arr[kind], i=slot:
                  s.dma_start(out=d, in_=a[i]), shw)
            slot_spill[sid] = (shw.h, shw.n)
            return (shw.h, shw.n)

    def pe_wait(mark):
        if mark is not None:
            P.wait_mark("tensor", mark)

    ln_state = {"xln_free": [None, None], "ytmb_free": None, "pstr_free": None}

    def emit_ln(src, tt):
        pe_wait(psd_free[0]); pe_wait(psd_free[1])
        pe_wait(psgu_free[0]); pe_wait(psgu_free[1])
        P.wait("vector", pem)
        for s in range(NSUB):
            r0 = tt * TT + s * 128
            xb = xln[s % 2]
            if ln_state["xln_free"][s % 2] is not None:
                P.wait_mark("sync", ln_state["xln_free"][s % 2])
            P.dma("sync", lambda e, r0=r0, xb=xb: e.dma_start(out=xb[:, :],
                                                              in_=src[r0:r0 + 128, :]), q_xl[s % 2])
            P.wait("vector", q_xl[s % 2])
            for i in range(4):
                P.emit("vector", lambda v, i=i, xb=xb: v.bn_stats(stats[:, i * 6:(i + 1) * 6],
                                                                  xb[:, i * 512:(i + 1) * 512]),
                       chain=(i == 0))
            P.emit("vector", lambda v: v.bn_aggr(mv[:, :], stats[:, 0:24]))
            P.wait_mark("scalar", P.mark("vector"))
            P.emit("scalar", lambda sc: sc.activation(out=rstd[:, :], in_=mv[:, 1:2],
                                                      func=AF.Sqrt, bias=cpool[:, 0:1]))
            P.wait_mark("vector", P.mark("scalar"))
            P.emit("vector", lambda v: v.reciprocal(rstd[:, :], rstd[:, :]))
            if ln_state["ytmb_free"] is not None:
                P.wait_mark("vector", ln_state["ytmb_free"])
            P.emit("vector", lambda v, xb=xb: v.tensor_scalar(ytmb[:, :], xb[:, :], mv[:, 0:1],
                                                              rstd[:, 0:1], ALU.subtract, ALU.mult))
            ln_state["xln_free"][s % 2] = P.mark("vector")
            pe_wait(P.mark("vector"))
            if ln_state["pstr_free"] is not None:
                pe_wait(ln_state["pstr_free"])
            def trans(t, s=s):
                r = None
                for kc in range(NKC):
                    r = tpose(t, pstr[:, kc * 128:(kc + 1) * 128].rearrange("p (a) -> p a"),
                                    ytmb[:, kc * 128:(kc + 1) * 128], cI16[:, :])
                return r
            P.emit("tensor", trans, pem, 1)
            ln_state["ytmb_free"] = (pem.h, pem.n)
            P.wait("vector", pem)
            P.emit("vector", lambda v, s=s: v.tensor_copy(
                yfm[:, :, s * 128:(s + 1) * 128],
                pstr[:, 0:2048].rearrange("p (kc f) -> p kc f", kc=NKC)))
            ln_state["pstr_free"] = P.mark("vector")

    psgu_free = [None, None]
    psd_free = [None, None]
    upw_free = [None, None, None, None]
    wdh_free = [None] * 9
    wbsb_free = [None]

    def emit_ffn(src, dst, tt):
        parts = dev_parts or ("ln", "router", "expert", "epi")
        emit_ln(src, tt)
        yfm_ready = P.mark("vector")
        if "router" not in parts:
            return

        pe_wait(yfm_ready)
        pe_wait(psd_free[0]); pe_wait(psd_free[1])
        pe_wait(psgu_free[0]); pe_wait(psgu_free[1])
        def zmm(t):
            r = None
            for kc in range(NKC):
                for half in range(2):
                    r = t.matmul(psz[:, half * 512:(half + 1) * 512], wrsb[:, kc, :],
                                 yfm[:, kc, half * 512:(half + 1) * 512],
                                 start=(kc == 0), stop=(kc == NKC - 1))
            return r
        P.emit("tensor", zmm, pem, 1)
        P.wait("vector", pem)
        P.emit("vector", lambda v: v.tensor_copy(zfm[:, :], psz))
        zfm_ready = P.mark("vector")
        for s in range(NSUB):
            pe_wait(P.mark("vector"))
            P.emit("tensor", lambda t, s=s: tpose(t, pszt, zfm[:, s * 128:(s + 1) * 128],
                                                        cIf[0:NE, 0:NE]), pem, 1)
            P.wait("vector", pem)
            P.emit("vector", lambda v: v.tensor_copy(ztm[:, :], pszt))
            P.emit("vector", lambda v: v.max(top8[:, :], ztm[:, :]))
            P.emit("vector", lambda v: v.tensor_scalar(nm1[:, :], top8[:, 0:1], -1.0, None,
                                                       ALU.mult))
            P.wait_mark("scalar", P.mark("vector"))
            P.emit("scalar", lambda sc: sc.activation(out=wtm[:, :], in_=ztm[:, :], func=AF.Exp,
                                                      bias=nm1[:, 0:1], accum_out=sume[:, :]))
            P.wait_mark("vector", P.mark("scalar"))
            P.emit("vector", lambda v: v.reciprocal(sume[:, :], sume[:, :]))
            P.emit("vector", lambda v: v.tensor_scalar(rmask[:, :], ztm[:, :], top8[:, 1:2],
                                                       None, ALU.is_ge))
            P.emit("vector", lambda v: v.scalar_tensor_tensor(wtm[:, :], wtm[:, :], sume[:, 0:1],
                                                              rmask[:, :], ALU.mult, ALU.mult))
            P.emit("vector", lambda v, s=s: v.tensor_copy(wall[:, s * NE:(s + 1) * NE],
                                                           wtm[:, :]))
        wfm_ready = P.mark("vector")
        psgu_free[0] = wfm_ready
        psgu_free[1] = wfm_ready
        if "expert" not in parts:
            return

        after_down = None
        up_done = None
        sgscr_war = [None, None]
        npe = NPE
        for pp in (parts or ()):
            if isinstance(pp, str) and pp.startswith("npe="):
                npe = int(pp[4:])
        for pe in range(npe):
            routed = pe >= 2
            if routed:
                e = pe - 2
                gkeys = [("wg", e, fc) for fc in range(NF)]
                ukeys = [("wu", e, fc) for fc in range(NF)]
                dkeys = [("wd", e, fc) for fc in range(NF)]
                gsrc = lambda fc, e=e: pk_wg[e, fc]
                usrc = lambda fc, e=e: pk_wu[e, fc]
                dsrc = lambda fc, e=e: pk_wd[e, fc]
            else:
                base = pe * NF
                gkeys = [("wsg", 0, base + fc) for fc in range(NF)]
                ukeys = [("wsu", 0, base + fc) for fc in range(NF)]
                dkeys = [("wsd", 0, base + fc) for fc in range(NF)]
                gsrc = lambda fc, base=base: pk_wsg[base + fc]
                usrc = lambda fc, base=base: pk_wsu[base + fc]
                dsrc = lambda fc, base=base: pk_wsd[base + fc]


            for fc in range(NF):
                gb = fc % 2
                ub = 2 + fc % 2
                wg_rdy = load_block(gkeys[fc], "up", upw[gb][:, :, :].rearrange("p a b -> p (a b)"),
                                    lambda g, d, fn=gsrc, fc=fc: g.dma_start(
                                        out=d.rearrange("p (a b) -> p a b", a=NKC),
                                        in_=fn(fc).rearrange("(kc kp) f -> kp kc f", kp=128)),
                                    slot_sem=supw[gb], free_mark=upw_free[gb])
                wu_rdy = load_block(ukeys[fc], "up", upw[ub][:, :, :].rearrange("p a b -> p (a b)"),
                                    lambda g, d, fn=usrc, fc=fc: g.dma_start(
                                        out=d.rearrange("p (a b) -> p a b", a=NKC),
                                        in_=fn(fc).rearrange("(kc kp) f -> kp kc f", kp=128)),
                                    slot_sem=supw[ub], free_mark=upw_free[ub])
                P.wait_mark("tensor", wg_rdy)
                P.wait_mark("tensor", wu_rdy)
                pe_wait(psgu_free[fc % 2])
                if fc == 0:
                    pe_wait(after_down)
                def gumm(t, fc=fc, gb=gb, ub=ub):
                    r = None
                    for kc in range(NKC):
                        for half in range(2):
                            r = t.matmul(psg[fc % 2][:, half * 512:(half + 1) * 512],
                                         upw[gb][:, kc, :],
                                         yfm[:, kc, half * 512:(half + 1) * 512],
                                         start=(kc == 0), stop=(kc == NKC - 1))
                            r = t.matmul(psu[fc % 2][:, half * 512:(half + 1) * 512],
                                         upw[ub][:, kc, :],
                                         yfm[:, kc, half * 512:(half + 1) * 512],
                                         start=(kc == 0), stop=(kc == NKC - 1))
                    return r
                P.emit("tensor", gumm, pem, 1)
                upw_free[gb] = (pem.h, pem.n)
                upw_free[ub] = (pem.h, pem.n)
                P.wait("scalar", pem)
                sg2 = sgscr[fc % 2]
                if sgscr_war[fc % 2] is not None:
                    P.wait_mark("scalar", sgscr_war[fc % 2])
                P.emit("scalar", lambda sc, fc=fc, sg2=sg2: sc.activation(
                    out=sg2[:, :], in_=psg[fc % 2], func=AF.Silu), chain=False)
                P.wait_mark("vector", P.mark("scalar"))
                P.emit("vector", lambda v, fc=fc, sg2=sg2: v.tensor_mul(hreg[:, fc, :],
                                                                        sg2[:, :],
                                                                        psu[fc % 2]),
                       chain=False)
                psgu_free[fc % 2] = P.mark("vector")
                sgscr_war[fc % 2] = P.mark("vector")
            up_done = P.mark("vector")

            for hf, (fcl, slots) in enumerate(((list(range(6)), list(range(6))),
                                               (list(range(6, NF)), [6, 7, 8, 0, 1]))):
                rdys = []
                for j, fc in zip(slots, fcl):
                    rdy = load_block(dkeys[fc], "dn", wdh[j],
                                     lambda g, d, fn=dsrc, fc=fc: g.dma_start(out=d, in_=fn(fc)),
                                     slot_sem=swdh[j], free_mark=wdh_free[j])
                    rdys.append(rdy)
                for s in range(NSUB):
                    if s == 0:
                        for rdy in rdys:
                            P.wait_mark("tensor", rdy)
                        pe_wait(up_done)
                        if hf == 0 and pe == 0:
                            pe_wait(psgu_free[0]); pe_wait(psgu_free[1])
                    pe_wait(psd_free[s % 2])
                    def dmm(t, s=s, fcl=fcl, slots=slots):
                        r = None
                        for i, (j, fc) in enumerate(zip(slots, fcl)):
                            for n in range(4):
                                r = t.matmul(psd[s % 2][:, n * 512:(n + 1) * 512],
                                             hreg[:, fc, s * 128:(s + 1) * 128],
                                             wdh[j][:, n * 512:(n + 1) * 512],
                                             start=(i == 0), stop=(i == len(fcl) - 1))
                        return r
                    P.emit("tensor", dmm, pem, 1)
                    if s == NSUB - 1:
                        for j in slots:
                            wdh_free[j] = (pem.h, pem.n)
                    P.wait("vector", pem)
                    if pe == 0 and hf == 0:
                        P.emit("vector", lambda v, s=s: v.tensor_copy(racc[:, s, :], psd[s % 2]),
                               chain=False)
                    elif not routed or hf == 0 and False:
                        P.emit("vector", lambda v, s=s: v.tensor_add(racc[:, s, :], psd[s % 2],
                                                                     racc[:, s, :]), chain=False)
                    elif not routed:
                        pass
                    else:
                        P.emit("vector", lambda v, s=s, e=e: v.scalar_tensor_tensor(
                            racc[:, s, :], psd[s % 2], wall[:, s * NE + e:s * NE + e + 1],
                            racc[:, s, :], ALU.mult, ALU.add), chain=False)
                    psd_free[s % 2] = P.mark("vector")
            after_down = P.mark("vector")

        if "epi" not in parts:
            return
        for s in range(NSUB):
            r0 = tt * TT + s * 128
            xb = xln[s % 2]
            if ln_state["xln_free"][s % 2] is not None:
                P.wait_mark("sync", ln_state["xln_free"][s % 2])
            P.dma("sync", lambda e, r0=r0, xb=xb: e.dma_start(out=xb[:, :],
                                                              in_=src[r0:r0 + 128, :]), q_xl[s % 2])
            P.wait("vector", q_xl[s % 2])
            P.emit("vector", lambda v, s=s, xb=xb: v.scalar_tensor_tensor(
                racc[:, s, :], racc[:, s, :], 0.5, xb[:, :], ALU.mult, ALU.add), chain=False)
            ln_state["xln_free"][s % 2] = P.mark("vector")
            P.wait_mark("scalar", P.mark("vector"))
            P.dma("scalar", lambda e, s=s, r0=r0: e.dma_start(out=dst[r0:r0 + 128, :],
                                                              in_=racc[:, s, :]), q_out)

    miw_free = [None, None]
    opb_free = [None, None]
    s_rep = [dsem("s_rep0"), dsem("s_rep1")]
    s_ea = dsem("s_ea")
    q_acs = dsem("q_acs")
    q_sg = dsem("q_sg")
    q_ys = dsem("q_ys")
    s_ysr = [dsem("s_ysr0"), dsem("s_ysr1")]
    s_sgu = [dsem("s_sgu0"), dsem("s_sgu1")]
    q_xe = dsem("q_xe")
    s_wb = dsem("s_wb")
    q_wsp = dsem("q_wsp")
    mixm = {}

    def emit_mixer(src, dst, tt):
        mparts = dev_parts or ("inx", "dtp", "gate", "conv", "ssd", "gating", "op")
        emit_ln(src, tt)

        if "inx" not in mparts:
            return
        P.wait("vector", q_out)
        P.emit("vector", lambda v: v.tensor_copy(hwork[:, 1:35, 0:3], carry[:, :, :]))
        carry_pre = P.mark("vector")
        px = psg
        px_free = [ln_state["pstr_free"], ln_state["pstr_free"]]
        for oc in range(NCONV):
            bi = oc % 2
            rdy = load_block(("inx", 0, oc), "up", miw[bi][:, :, :].rearrange("p a b -> p (a b)"),
                             lambda g, d, oc=oc: g.dma_start(
                                 out=d.rearrange("p (a b) -> p a b", a=NKC),
                                 in_=pk_inx[oc].rearrange("(kc kp) f -> kp kc f", kp=128)),
                             slot_sem=smiw[bi], free_mark=miw_free[bi])
            P.wait_mark("tensor", rdy)
            pe_wait(px_free[bi])
            def xmm(t, bi=bi):
                r = None
                for kc in range(NKC):
                    for half in range(2):
                        r = t.matmul(px[bi][:, half * 512:(half + 1) * 512], miw[bi][:, kc, :],
                                     yfm[:, kc, half * 512:(half + 1) * 512],
                                     start=(kc == 0), stop=(kc == NKC - 1))
                return r
            P.emit("tensor", xmm, pem, 1)
            miw_free[bi] = (pem.h, pem.n)
            P.wait("vector", pem)
            if oc < 2:
                P.wait_mark("vector", carry_pre)
            P.emit("vector", lambda v, oc=oc, bi=bi: v.tensor_copy(hwork[:, oc + 1, 3:1027],
                                                                   px[bi]))
            px_free[bi] = P.mark("vector")
        if "dtp" not in mparts:
            return
        dtw = miw[0][:, :, :].rearrange("p a b -> p (a b)")[:, 0:1024].rearrange(
            "p (a b) -> p a b", a=NKC)
        rdy = load_block(("indt", 0, 0), "dt",
                         miw[0][:, :, :].rearrange("p a b -> p (a b)")[:, 0:1024],
                         lambda g, d: g.dma_start(
                             out=d.rearrange("p (a b) -> p a b", a=NKC),
                             in_=pk_indt.rearrange("(kc kp) h -> kp kc h", kp=128)),
                         slot_sem=smiw[0], free_mark=miw_free[0])
        P.wait_mark("tensor", rdy)
        pe_wait(px_free[1])
        def dtmm(t):
            r = None
            for kc in range(NKC):
                for half in range(2):
                    r = t.matmul(psdt[:, half * 512:(half + 1) * 512], dtw[:, kc, :],
                                 yfm[:, kc, half * 512:(half + 1) * 512],
                                 start=(kc == 0), stop=(kc == NKC - 1))
            return r
        P.emit("tensor", dtmm, pem, 1)
        miw_free[0] = (pem.h, pem.n)
        P.wait("vector", pem)
        P.emit("vector", lambda v: v.tensor_copy(dtraw[:, :], psdt))
        psdt_done = P.mark("vector")
        px_free[1] = psdt_done

        if "gate" not in mparts:
            return
        for oc in range(NGATE):
            bi = oc % 2
            rdy = load_block(("ing", 0, oc), "up", miw[bi][:, :, :].rearrange("p a b -> p (a b)"),
                             lambda g, d, oc=oc: g.dma_start(
                                 out=d.rearrange("p (a b) -> p a b", a=NKC),
                                 in_=pk_ing[oc].rearrange("(kc kp) f -> kp kc f", kp=128)),
                             slot_sem=smiw[bi], free_mark=miw_free[bi])
            P.wait_mark("tensor", rdy)
            pe_wait(px_free[bi])
            def gmm(t, bi=bi):
                r = None
                for kc in range(NKC):
                    for half in range(2):
                        r = t.matmul(px[bi][:, half * 512:(half + 1) * 512], miw[bi][:, kc, :],
                                     yfm[:, kc, half * 512:(half + 1) * 512],
                                     start=(kc == 0), stop=(kc == NKC - 1))
                return r
            P.emit("tensor", gmm, pem, 1)
            miw_free[bi] = (pem.h, pem.n)
            P.wait("scalar", pem)
            P.wait("scalar", q_sg)
            P.emit("scalar", lambda sc, oc=oc, bi=bi: sc.activation(
                out=ytmb[:, (oc % 2) * 1024:(oc % 2 + 1) * 1024],
                in_=px[bi], func=AF.Silu))
            px_free[bi] = P.mark("scalar")
            P.wait_mark("scalar", P.mark("scalar"))
            P.dma("scalar", lambda e, oc=oc: e.dma_start(
                out=gate_scr[oc], in_=ytmb[:, (oc % 2) * 1024:(oc % 2 + 1) * 1024]),
                q_sg)

        if "conv" not in mparts:
            return
        P.wait_mark("vector", px_free[0]); P.wait_mark("vector", px_free[1])
        P.emit("vector", lambda v: v.tensor_copy(carry[:, :, :], hwork[:, 1:35, 1024:1027]))
        for oc in range(NCONV):
            P.emit("vector", lambda v, oc=oc: v.tensor_scalar(
                convacc[:, :], hwork[:, oc + 1, 0:1024], convw[:, oc, 0:1], None, ALU.mult))
            for kk in range(1, CONVK):
                P.emit("vector", lambda v, oc=oc, kk=kk: v.scalar_tensor_tensor(
                    convacc[:, :], hwork[:, oc + 1, kk:kk + 1024], convw[:, oc, kk:kk + 1],
                    convacc[:, :], ALU.mult, ALU.add))
            P.wait_mark("scalar", P.mark("vector"))
            P.emit("scalar", lambda sc, oc=oc: sc.activation(
                out=hwork[:, oc, 0:1024], in_=convacc[:, :], func=AF.Silu,
                bias=convb[:, oc, 0:1]))
            if oc + 1 < NCONV:
                P.wait_mark("vector", P.mark("scalar"))
        conv_done = P.mark("scalar")

        if "ssd" not in mparts:
            return
        nchunk = NSUB
        for pp in mparts:
            if isinstance(pp, str) and pp.startswith("nch="):
                nchunk = int(pp[4:])
        if tt == 0:
            P.wait("vector", pem)
            P.emit("vector", lambda v: v.memset(Rst[:, :], 0.0))
        for k in range(nchunk):
            cols = slice(k * 128, (k + 1) * 128)
            P.wait_mark("scalar", psdt_done)
            P.emit("scalar", lambda sc, cols=cols: sc.activation(
                out=spt[:, :], in_=dtraw[:, cols], func=AF.Abs, bias=dtb[:, 0:1]))
            P.emit("scalar", lambda sc: sc.activation(out=spt[:, :], in_=spt[:, :],
                                                      func=AF.Exp, scale=-1.0))
            P.emit("scalar", lambda sc: sc.activation(out=spt[:, :], in_=spt[:, :],
                                                      func=AF.Ln, bias=cpool[0:NH, 1:2]))
            P.emit("scalar", lambda sc, cols=cols: sc.activation(
                out=sdt[:, :], in_=dtraw[:, cols], func=AF.Relu, bias=dtb[:, 0:1]))
            P.wait_mark("vector", P.mark("scalar"))
            P.emit("vector", lambda v: v.tensor_add(sdt[:, :], sdt[:, :], spt[:, :]))
            P.emit("vector", lambda v: v.tensor_scalar(acs[:, :], sdt[:, :], negA[:, 0:1],
                                                       None, ALU.mult))
            bufs = [acs, acs2]
            for i, sh in enumerate((1, 2, 4, 8, 16, 32, 64)):
                a, b = bufs[i % 2], bufs[(i + 1) % 2]
                P.emit("vector", lambda v, a=a, b=b, sh=sh: v.tensor_copy(b[:, 0:sh], a[:, 0:sh]))
                P.emit("vector", lambda v, a=a, b=b, sh=sh: v.tensor_add(
                    b[:, sh:], a[:, sh:], a[:, 0:CH - sh]))
            acsf = bufs[1]
            P.wait_mark("scalar", P.mark("vector"))
            P.emit("scalar", lambda sc, acsf=acsf: sc.activation(
                out=spt[:, :], in_=acsf[:, :], func=AF.Exp, scale=-1.0,
                bias=acsf[:, 127:128]))
            P.wait_mark("vector", P.mark("scalar"))
            P.emit("vector", lambda v: v.tensor_mul(wst[:, :], spt[:, :], sdt[:, :]))
            wst_mark = P.mark("vector")
            P.wait("scalar", q_acs)
            P.dma("scalar", lambda e, k=k, acsf=acsf: e.dma_start(out=rowscr[k % 2],
                                                                  in_=acsf[:, :]), q_acs)
            P.dma("scalar", lambda e, k=k, acsf=acsf: e.dma_start(
                out=rowlast[k % 2], in_=acsf[:, 127:128], allow_slow_non_contiguous=True), q_acs)
            pe_wait(wst_mark)
            pe_wait(mixm.get("psT_free"))
            P.emit("tensor", lambda t, acsf=acsf: tpose(t, psTa, acsf[:, :],
                                                              cIf[0:NH, 0:NH]))
            P.emit("tensor", lambda t: tpose(t, psTb, sdt[:, :], cIf[0:NH, 0:NH]))
            P.emit("tensor", lambda t: tpose(t, psTc, wst[:, :], cIf[0:NH, 0:NH]), pem, 1)
            P.wait("vector", pem)
            P.emit("vector", lambda v: v.tensor_copy(acsT[:, :], psTa))
            P.emit("vector", lambda v: v.tensor_copy(dtT[:, :], psTb))
            P.emit("vector", lambda v: v.tensor_copy(wstT[:, :], psTc))
            mixm["psT_free"] = P.mark("vector")
            P.wait_mark("scalar", mixm["psT_free"])
            P.emit("scalar", lambda sc: sc.activation(out=eaT[:, :], in_=acsT[:, :],
                                                      func=AF.Exp))
            eaT_mark = P.mark("scalar")
            P.wait("sync", q_acs)
            P.wait_mark("sync", mixm.get("easum_free"))
            P.dma("sync", lambda e, k=k: e.dma_start(
                out=easum[:, :],
                in_=rowlast[k % 2].rearrange("h o -> (h o)")[None, :].partition_broadcast(128)[:, 0, :]),
                s_ea)
            P.wait("scalar", s_ea)
            P.emit("scalar", lambda sc: sc.activation(out=easum[:, :], in_=easum[:, :],
                                                      func=AF.Exp))
            easum_mark = P.mark("scalar")
            P.wait_mark("tensor", conv_done if k == 0 else None)
            pe_wait(psdt_done)
            pe_wait(mixm.get("pstr2_free"))
            for rnd in range(2):
                if rnd > 0:
                    pe_wait(P.mark("vector"))
                def xtr(t, rnd=rnd, cols=cols):
                    r = None
                    for j in range(16):
                        fc = rnd * 16 + j
                        r = tpose(t, pstr2[:, j * 128:(j + 1) * 128],
                                        hwork[:, fc, cols], cI16[:, :])
                    return r
                P.emit("tensor", xtr, pem, 1)
                P.wait("vector", pem)
                P.emit("vector", lambda v, rnd=rnd: v.tensor_copy(
                    xT[:, rnd * 2048:(rnd + 1) * 2048],
                    pstr2[:, 0:2048]))
            pe_wait(P.mark("vector"))
            P.emit("tensor", lambda t, cols=cols: tpose(t, pstr2[:, 0:128],
                                                              hwork[:, 32, cols], cI16[:, :]),
                   pem, 1)
            P.wait("vector", pem)
            P.emit("vector", lambda v: v.tensor_copy(bT[:, :], pstr2[:, 0:128]))
            mixm["pstr2_free"] = P.mark("vector")
            P.emit("vector", lambda v: v.tensor_tensor(
                xdt.rearrange("p (h d) -> p h d", h=NH),
                xT.rearrange("p (h d) -> p h d", h=NH),
                bcast_last(dtT[:, :], HD), ALU.mult))
            xdt_mark = P.mark("vector")
            pe_wait(mixm.get("psG_free"))
            P.emit("tensor", lambda t, cols=cols: t.matmul(psG[:, 0:128], hwork[:, 32, cols],
                                                           hwork[:, 33, cols],
                                                           start=True, stop=True), pem, 1)
            P.wait("vector", pem)
            P.emit("vector", lambda v: v.tensor_mul(gmask[:, :], psG[:, 0:128], ctril[:, :]))
            mixm["psG_free"] = P.mark("vector")
            gmask_mark = mixm["psG_free"]

            P.wait("vector", q_ys)
            for g in range(8):
                gb = g % 2
                P.wait("sync", q_acs)
                P.wait_mark("sync", mixm.get(f"rep_free{gb}"))
                P.dma("sync", lambda e, k=k, g=g: e.dma_start(
                    out=rep[g % 2][:, :],
                    in_=rowscr[k % 2].rearrange("h t -> (h t)")[None, g * 1024:(g + 1) * 1024]
                        .partition_broadcast(128)[:, 0, :]), s_rep[gb])
                P.wait_mark("vector", mixm.get(f"rbg_free{gb}"))
                P.emit("vector", lambda v, g=g, gb=gb: v.tensor_copy(
                    rbg[:, gb * 512:(gb + 1) * 512], Rst[:, g * 512:(g + 1) * 512]))
                rbg_mark = P.mark("vector")
                pe_wait(rbg_mark)
                pe_wait(mixm.get("psYA_free"))
                def yoff(t, cols=cols, g=g, gb=gb):
                    r = None
                    for j in range(8):
                        r = t.matmul(psYA[:, j * 64:(j + 1) * 64], hwork[:, 33, cols],
                                     rbg[:, gb * 512 + j * 64:gb * 512 + (j + 1) * 64],
                                     start=(j == 0), stop=(j == 7))
                    return r
                P.emit("tensor", yoff, pem, 1)
                mixm[f"rbg_free{gb}"] = (pem.h, pem.n)
                P.wait("vector", s_rep[gb])
                P.emit("vector", lambda v, g=g, gb=gb: v.scalar_tensor_tensor(
                    dsb[:, :].rearrange("p (h t) -> p h t", h=8),
                    rep[g % 2][:, :].rearrange("p (h t) -> p h t", h=8), 1.0,
                    bcast_last(acsT[:, g * 8:(g + 1) * 8], CH),
                    ALU.mult, ALU.subtract))
                mixm[f"rep_free{gb}"] = P.mark("vector")
                P.wait_mark("scalar", P.mark("vector"))
                P.wait_mark("scalar", eaT_mark if g == 0 else None)
                P.emit("scalar", lambda sc: sc.activation(out=dsb[:, :], in_=dsb[:, :],
                                                          func=AF.Exp))
                P.wait_mark("vector", P.mark("scalar"))
                P.wait_mark("vector", mixm.get(f"glb_free{gb}"))
                P.emit("vector", lambda v, gb=gb: v.scalar_tensor_tensor(
                    glb[gb][:, :].rearrange("p (h t) -> p h t", h=8),
                    dsb[:, :].rearrange("p (h t) -> p h t", h=8), 1.0,
                    bcast(gmask[:, :], 0, 8), ALU.min, ALU.mult))
                glb_mark = P.mark("vector")
                pe_wait(glb_mark)
                if g == 0:
                    pe_wait(xdt_mark)
                    pe_wait(mixm.get("psYB_free"))
                def ydiag(t, g=g, gb=gb):
                    r = None
                    for j in range(8):
                        r = t.matmul(psYB[:, j * 64:(j + 1) * 64],
                                     glb[gb][:, j * 128:(j + 1) * 128],
                                     xdt[:, (g * 8 + j) * 64:(g * 8 + j + 1) * 64],
                                     start=(j == 0), stop=(j == 7))
                    return r
                P.emit("tensor", ydiag, pem, 1)
                mixm[f"glb_free{gb}"] = (pem.h, pem.n)
                ymm_mark = (pem.h, pem.n)
                P.wait("vector", pem)
                P.emit("vector", lambda v, g=g: v.tensor_tensor(
                    yevt[:, :].rearrange("p (h d) -> p h d", h=8),
                    psYA.rearrange("p (h d) -> p h d", h=8),
                    bcast_last(eaT[:, g * 8:(g + 1) * 8], HD), ALU.mult))
                mixm["psYA_free"] = P.mark("vector")
                P.emit("vector", lambda v: v.scalar_tensor_tensor(
                    yevt[:, :], psYB, 1.0, yevt[:, :], ALU.mult, ALU.add))
                mixm["psYB_free"] = P.mark("vector")
                P.emit("vector", lambda v, g=g: v.tensor_tensor(
                    dres[:, :].rearrange("p (h d) -> p h d", h=8),
                    xT[:, g * 512:(g + 1) * 512].rearrange("p (h d) -> p h d", h=8),
                    bcast_last(dD[:, g * 8:(g + 1) * 8], HD), ALU.mult))
                P.emit("vector", lambda v, g=g: v.tensor_add(
                    yssdb[:, g * 512:(g + 1) * 512], yevt[:, :], dres[:, :]))
            P.wait("vector", pem)
            P.emit("vector", lambda v: v.tensor_tensor(
                xw.rearrange("p (h d) -> p h d", h=NH),
                xT.rearrange("p (h d) -> p h d", h=NH),
                bcast_last(wstT[:, :], HD), ALU.mult))
            xw_mark = P.mark("vector")
            psS_rot = [psS, psYA]
            for g in range(8):
                pe_wait(xw_mark if g == 0 else None)
                pe_wait(mixm.get(f"psS_free{g % 2}"))
                if g == 0:
                    pe_wait(mixm.get("psYA_free"))
                def smm(t, cols=cols, g=g):
                    r = None
                    for j in range(8):
                        r = t.matmul(psS_rot[g % 2][:, j * 64:(j + 1) * 64], bT[:, :],
                                     xw[:, (g * 8 + j) * 64:(g * 8 + j + 1) * 64],
                                     start=(j == 0), stop=(j == 7))
                    return r
                P.emit("tensor", smm, pem, 1)
                P.wait("vector", pem)
                P.wait_mark("vector", easum_mark if g == 0 else None)
                P.emit("vector", lambda v, g=g: v.tensor_tensor(
                    rupd[:, :].rearrange("p (h d) -> p h d", h=8),
                    Rst[:, g * 512:(g + 1) * 512].rearrange("p (h d) -> p h d", h=8),
                    bcast_last(easum[:, g * 8:(g + 1) * 8], HD), ALU.mult))
                P.emit("vector", lambda v, g=g: v.tensor_add(
                    Rst[:, g * 512:(g + 1) * 512], rupd[:, :], psS_rot[g % 2]))
                mixm[f"psS_free{g % 2}"] = P.mark("vector")
            mixm["easum_free"] = P.mark("vector")
            P.wait_mark("scalar", P.mark("vector"))
            P.dma("scalar", lambda e, k=k: e.dma_start(out=yssd_scr[k], in_=yssdb[:, :]), q_ys)

        if "gating" not in mparts:
            return
        phase_barrier()
        for k2 in range(NSUB):
            kb = k2 % 2
            P.wait("sync", q_ys)
            P.wait_mark("sync", mixm.get(f"ysr_free{kb}"))
            P.dma("sync", lambda e, k2=k2, kb=kb: e.dma_start(out=ysr[kb][:, :],
                                                              in_=yssd_scr[k2]), s_ysr[kb])
            P.wait("sync", q_sg)
            P.wait_mark("sync", mixm.get(f"sgu_free{kb}"))
            P.dma("sync", lambda e, k2=k2, kb=kb: e.dma_start(
                out=sgu[kb][:, :, :],
                in_=gate_scr[:, :, k2 * 128:(k2 + 1) * 128].rearrange("oc p t -> p oc t")),
                s_sgu[kb])
            P.wait("tensor", s_sgu[kb])
            pe_wait(mixm.get("sgT_free"))
            for rnd in range(2):
                if rnd > 0:
                    pe_wait(P.mark("vector"))
                def str_(t, rnd=rnd, kb=kb):
                    r = None
                    for j in range(16):
                        fc = rnd * 16 + j
                        r = tpose(t, pstr[:, (rnd * 16 + j) * 128:(rnd * 16 + j + 1) * 128],
                                        sgu[kb][:, fc, :], cI16[:, :])
                    return r
                P.emit("tensor", str_, pem, 1)
            P.wait("vector", pem)
            P.emit("vector", lambda v: v.tensor_copy(sgT[:, :], pstr[:, 0:4096]))
            mixm[f"sgu_free{kb}"] = P.mark("vector")
            mixm["sgT_free"] = P.mark("vector")
            P.wait("vector", s_ysr[kb])
            P.emit("vector", lambda v, kb=kb: v.tensor_mul(yfb[:, :], ysr[kb][:, :], sgT[:, :]))
            mixm[f"ysr_free{kb}"] = P.mark("vector")
            P.wait_mark("scalar", P.mark("vector"))
            P.emit("scalar", lambda sc: sc.activation(out=sqs[:, :], in_=yfb[:, :],
                                                      func=AF.Square, accum_out=s2c[:, :]))
            P.wait_mark("vector", P.mark("scalar"))
            P.emit("vector", lambda v: v.tensor_scalar(s2c[:, :], s2c[:, :], 1.0 / DIN, None,
                                                       ALU.mult))
            P.wait_mark("scalar", P.mark("vector"))
            P.emit("scalar", lambda sc: sc.activation(out=s2c[:, :], in_=s2c[:, :],
                                                      func=AF.Sqrt, bias=cpool[:, 0:1]))
            P.wait_mark("vector", P.mark("scalar"))
            P.emit("vector", lambda v: v.reciprocal(rstd2[:, :], s2c[:, :]))
            P.emit("vector", lambda v: v.tensor_scalar(yfb[:, :], yfb[:, :], rstd2[:, 0:1],
                                                       None, ALU.mult))
            yn_mark = P.mark("vector")
            pe_wait(yn_mark)
            pe_wait(mixm.get("pstr2_free"))
            for rnd in range(2):
                if rnd > 0:
                    pe_wait(P.mark("vector"))
                def ytr(t, rnd=rnd):
                    r = None
                    for j in range(16):
                        r = tpose(t, pstr2[:, j * 128:(j + 1) * 128],
                                        yfb[:, (rnd * 16 + j) * 128:(rnd * 16 + j + 1) * 128],
                                        cI16[:, :])
                    return r
                P.emit("tensor", ytr, pem, 1)
                P.wait("vector", pem)
                P.emit("vector", lambda v, rnd=rnd, k2=k2: v.tensor_copy(
                    ynfm[:, rnd * 16:(rnd + 1) * 16, k2 * 128:(k2 + 1) * 128],
                    pstr2[:, 0:2048].rearrange("p (fc f) -> p fc f", fc=16)))
            mixm["pstr2_free"] = P.mark("vector")

        if "op" not in mparts:
            return
        phase_barrier()
        for s2 in range(4):
            for di in range(NGATE):
                bi = di % 2
                rdy = load_block(("op", 0, di), "dn", opbuf[bi],
                                 lambda g, d, di=di: g.dma_start(out=d, in_=pk_op[di]),
                                 slot_sem=sopb[bi],
                                 postproc=(lambda bi=bi, di=di: P.emit(
                                     "vector", lambda v: v.tensor_scalar(
                                         opbuf[bi][:, :], opbuf[bi][:, :], nwfm[:, di:di + 1],
                                         None, ALU.mult))),
                                 free_mark=opb_free[bi])
                P.wait_mark("tensor", rdy)
                if di < 2:
                    pe_wait(psd_free[0]); pe_wait(psd_free[1])
                def opmm(t, s2=s2, di=di, bi=bi):
                    r = None
                    for ss in range(2):
                        sub = 2 * s2 + ss
                        for n in range(4):
                            r = t.matmul(psd[ss][:, n * 512:(n + 1) * 512],
                                         ynfm[:, di, sub * 128:(sub + 1) * 128],
                                         opbuf[bi][:, n * 512:(n + 1) * 512],
                                         start=(di == 0), stop=(di == NGATE - 1))
                    return r
                P.emit("tensor", opmm, pem, 1)
                opb_free[bi] = (pem.h, pem.n)
            for ss in range(2):
                sub = 2 * s2 + ss
                r0 = tt * TT + sub * 128
                P.wait("sync", q_out)
                P.wait_mark("sync", mixm.get("xep_free"))
                P.dma("sync", lambda e, r0=r0: e.dma_start(out=xep2[:, :],
                                                           in_=src[r0:r0 + 128, :]), q_xe)
                P.wait("vector", q_xe)
                P.wait("vector", pem)
                P.emit("vector", lambda v, ss=ss: v.tensor_add(xep2[:, :], psd[ss], xep2[:, :]))
                psd_free[ss] = P.mark("vector")
                mixm["xep_free"] = P.mark("vector")
                P.wait_mark("scalar", P.mark("vector"))
                P.dma("scalar", lambda e, r0=r0: e.dma_start(out=dst[r0:r0 + 128, :],
                                                             in_=xep2[:, :]), q_out)

    P.dma("sync", lambda e: e.dma_start(out=cIf[:, :], in_=idf_d[:, :]), q_x)
    P.dma("sync", lambda e: e.dma_start(out=ctril[:, :], in_=tril_d[:, :]), q_x)
    P.dma("sync", lambda e: e.dma_start(out=oh8[:, :], in_=oh8_d[:, :]), q_x)
    P.dma("sync", lambda e: e.dma_start(out=convw[:, :, :],
                                        in_=convw_d.rearrange("(fc p) k -> p fc k", p=128),
                                        allow_slow_non_contiguous=True), q_x)
    P.dma("sync", lambda e: e.dma_start(out=convb[:, :, :],
                                        in_=convb_d.rearrange("(fc p) o -> p fc o", p=128),
                                        allow_slow_non_contiguous=True), q_x)
    P.dma("sync", lambda e: e.dma_start(out=dtb[:, :], in_=dtb_d[:, :]), q_x)
    P.dma("sync", lambda e: e.dma_start(
        out=dD[:, :], in_=dvec_d.rearrange("o h -> (o h)")[None, :].partition_broadcast(128)[:, 0, :]), q_x)
    P.dma("sync", lambda e: e.dma_start(
        out=nwfm, in_=dataclasses.replace(nw_d, ap=[[1, 128], [128, NGATE]]),
        allow_slow_non_contiguous=True), q_x)
    P.dma("sync", lambda e: e.dma_start(out=negA[:, :], in_=alog_d[:, :]), q_x)
    P.dma("gpsimd", lambda g: g.dma_start(out=cI16[:, :], in_=id16_d[:, :]), q_w)
    P.dma("gpsimd", lambda g: g.dma_start(
        out=wrsb[:, :, :], in_=pk_wr.rearrange("(kc kp) e -> kp kc e", kp=128)), q_w)
    P.emit("vector", lambda v: v.memset(cpool[:, 0:1], 1e-5), chain=False)
    P.emit("vector", lambda v: v.memset(cpool[:, 1:2], 1.0))
    P.emit("vector", lambda v: v.memset(carry[:, :, :], 0.0))
    P.wait("scalar", q_x)
    P.emit("scalar", lambda sc: sc.activation(out=negA[:, :], in_=negA[:, :], func=AF.Exp))
    P.wait_mark("vector", P.mark("scalar"))
    P.emit("vector", lambda v: v.tensor_scalar(negA[:, :], negA[:, :], -1.0, None, ALU.mult))
    phase_barrier()

    srcs = {"ffn1": (x_in, xmid1), "mixer": (xmid1, xmid2), "ffn2": (xmid2, out_d)}
    srcs[phases[-1]] = (srcs[phases[-1]][0], out_d)
    for ph in phases:
        for tt in tiles:
            s_, d_ = srcs[ph]
            if ph == "mixer":
                emit_mixer(s_, d_, tt)
            else:
                emit_ffn(s_, d_, tt)
            phase_barrier()

    P.wait("scalar", q_out)

    with nc.Block() as block:
        P.run(block)
    return nc, es



def _pack(inputs):
    f32 = np.float32
    c = np.ascontiguousarray
    ipw = np.asarray(inputs["L0_attn_in_proj_w"], f32)
    Wg = np.asarray(inputs["L0_ffn_Wg"], f32)
    Wu = np.asarray(inputs["L0_ffn_Wu"], f32)
    Wd = np.asarray(inputs["L0_ffn_Wd"], f32)
    Wsg = np.asarray(inputs["L0_ffn_Wsg"], f32)
    Wsu = np.asarray(inputs["L0_ffn_Wsu"], f32)
    Wsd = np.asarray(inputs["L0_ffn_Wsd"], f32)
    Wr = np.asarray(inputs["L0_ffn_Wr"], f32)
    opw = np.asarray(inputs["L0_attn_out_proj_w"], f32)
    oh8 = np.zeros((NE, NE * 128), f32)
    for e in range(NE):
        oh8[e, e * 128:(e + 1) * 128] = 1.0
    d = {
        "pk_wsg": c(Wsg.T.reshape(D, 22, 128).transpose(1, 0, 2)),
        "pk_wsu": c(Wsu.T.reshape(D, 22, 128).transpose(1, 0, 2)),
        "pk_wg": c(Wg.transpose(0, 2, 1).reshape(NE, D, NF, 128).transpose(0, 2, 1, 3)),
        "pk_wu": c(Wu.transpose(0, 2, 1).reshape(NE, D, NF, 128).transpose(0, 2, 1, 3)),
        "pk_wsd": c(Wsd.T.reshape(22, 128, D)),
        "pk_wd": c(Wd.transpose(0, 2, 1).reshape(NE, NF, 128, D)),
        "pk_wr": c(Wr.T),
        "pk_inx": c(ipw[DIN:DIN + CONVD].T.reshape(D, NCONV, 128).transpose(1, 0, 2)),
        "pk_ing": c(ipw[0:DIN].T.reshape(D, NGATE, 128).transpose(1, 0, 2)),
        "pk_indt": c(ipw[DIN + CONVD:].T),
        "pk_op": c(opw.T.reshape(NGATE, 128, D)),
        "convw": np.asarray(inputs["L0_attn_conv_w"], f32),
        "convb": np.asarray(inputs["L0_attn_conv_b"], f32).reshape(CONVD, 1),
        "dtb": np.asarray(inputs["L0_attn_dt_bias"], f32).reshape(NH, 1),
        "alog": np.asarray(inputs["L0_attn_A_log"], f32).reshape(NH, 1),
        "dvec": np.asarray(inputs["L0_attn_D"], f32).reshape(1, NH),
        "nw": np.asarray(inputs["L0_attn_norm_w"], f32).reshape(1, DIN),
        "id16": np.eye(128, dtype=ml_dtypes.bfloat16),
        "idf": np.eye(128, dtype=f32),
        "tril": np.triu(np.ones((128, 128), f32)),
        "oh8": oh8,
    }
    return d


_prog_cache = {}

def kernel(**inputs):
    key = "full"
    if key not in _prog_cache:
        _prog_cache[key] = build_program()
    nc, es = _prog_cache[key]
    shared = _pack(inputs)
    x = np.asarray(inputs["L0_attn_hidden_states"], np.float32)
    in_maps = []
    for i in range(B):
        m = dict(shared)
        m["x"] = np.ascontiguousarray(x[i])
        in_maps.append(m)
    res = run_bass_kernel_spmd(nc, in_maps, core_ids=list(range(B)))
    return np.stack([res.results[i]["out"] for i in range(B)], axis=0)
